# version 16
# baseline (speedup 1.0000x reference)
import numpy as np
import ml_dtypes
from contextlib import ExitStack

import concourse.bass as bass
import concourse.mybir as mybir
from concourse.bass_utils import run_bass_kernel_spmd

F32 = mybir.dt.float32
BF16 = mybir.dt.bfloat16
AF = mybir.ActivationFunctionType
ALU = mybir.AluOpType
BF = ml_dtypes.bfloat16

D = 1024
T = 4096
NH = 16
HD = 64
NE = 8
HE = 1408
NCC = 8
TILE = 512
NTILE = 8
QT = 1024
NQ = 4
TB = 512
NHC = 11
NPE = 10

ENGS = ("sync", "tensor", "vector", "scalar", "gpsimd")


class Sem:
    __slots__ = ("h", "n", "name", "waited_max", "issuer_synced")

    def __init__(self, h, name):
        self.h = h
        self.n = 0
        self.name = name
        self.waited_max = 0
        self.issuer_synced = 0


class Build:
    def __init__(self, nc, ctx):
        self.nc = nc
        self.ctx = ctx
        self.q = {k: [] for k in ENGS}

    def sem(self, name):
        return Sem(self.ctx.enter_context(self.nc.semaphore(name)), name)

    def op(self, eng, fn, waits=(), inc=None, amt=1):
        wl = tuple((s, int(v)) for s, v in waits if v > 0)
        for s, v in wl:
            assert v <= s.n, f"wait on future {s.name}:{v} > {s.n}"
            s.waited_max = max(s.waited_max, v)
        if inc is not None:
            inc.n += amt

        def run(e, fn=fn, wl=wl, inc=inc, amt=amt):
            for s, v in wl:
                e.wait_ge(s.h, v)
            inst = fn(e)
            if inc is not None:
                inst.then_inc(inc.h, amt)

        self.q[eng].append(run)

    def waits(self, eng, waits):
        wl = tuple((s, int(v)) for s, v in waits if v > 0)
        for s, v in wl:
            assert v <= s.n
            s.waited_max = max(s.waited_max, v)
        if wl:
            self.q[eng].append(lambda e, wl=wl: [e.wait_ge(s.h, v) for s, v in wl])

    def bump(self, eng, sem, amt=1):
        sem.n += amt
        self.q[eng].append(lambda e, sem=sem, amt=amt: e.sem_inc(sem.h, amt))

    def drain(self, eng):
        self.q[eng].append(lambda e: e.drain())

    def make_lanes(self, spec):
        self.lanes = {eng: [self.sem(f"q_{eng}{i}") for i in range(n)] for eng, n in spec.items()}
        self.lane_i = {eng: 0 for eng in spec}

    def dma(self, eng, dst, src, waits=()):
        lane = self.lanes[eng][self.lane_i[eng] % len(self.lanes[eng])]
        self.lane_i[eng] += 1
        extra = [(lane, lane.n)] if lane.n > lane.issuer_synced else []
        lane.issuer_synced = max(lane.issuer_synced, lane.n)
        self.op(eng, lambda e, d=dst, s=src: e.dma_start(d, s),
                waits=list(waits) + extra, inc=lane, amt=16)
        return (lane, lane.n)

    def lane_snap(self, eng):
        return [(lane, lane.n) for lane in self.lanes[eng] if lane.n > 0]

    def dma_multi(self, eng, pairs, waits=()):
        d = {}
        for i, (dst, srcv) in enumerate(pairs):
            s, v = self.dma(eng, dst, srcv, waits=waits if i == 0 else ())
            d[s] = max(d.get(s, 0), v)
        return list(d.items())


def build_program(debug=False, n_tiles=NTILE, n_q=NQ, stages=5):
    nc = bass.Bass("TRN2", target_bir_lowering=False)

    dr = lambda n, sh, dt, kind: nc.dram_tensor(n, sh, dt, kind=kind).ap()
    x_fm = dr("x_fm", [D, T], F32, "ExternalInput")
    wq_d = dr("wq", [D, D], F32, "ExternalInput")
    wk_d = dr("wk", [D, D], F32, "ExternalInput")
    wv_d = dr("wv", [D, D], F32, "ExternalInput")
    wo_d = dr("wo", [D, D], F32, "ExternalInput")
    gamma_d = dr("gamma_fm", [128, NCC], F32, "ExternalInput")
    wg_d = dr("wg", [D, NE], F32, "ExternalInput")
    w1_d = dr("w1", [NE, D, HE], F32, "ExternalInput")
    w2_d = dr("w2", [NE, D, HE], F32, "ExternalInput")
    w3_d = dr("w3", [NE, HE, D], F32, "ExternalInput")
    ws1_d = dr("ws1", [D, 2 * HE], F32, "ExternalInput")
    ws2_d = dr("ws2", [D, 2 * HE], F32, "ExternalInput")
    ws3_d = dr("ws3", [2 * HE, D], F32, "ExternalInput")
    cos_d = dr("cosT", [128, T], BF16, "ExternalInput")
    sin_d = dr("sinT", [128, T], BF16, "ExternalInput")
    perm_d = dr("permP", [128, 128], BF16, "ExternalInput")
    ident_d = dr("identB", [128, 128], BF16, "ExternalInput")
    selab_d = dr("selAB", [2, 256], BF16, "ExternalInput")
    selw_d = dr("selW", [16, NE * 128], BF16, "ExternalInput")
    maskhi_d = dr("maskhi2", [128, 512], BF16, "ExternalInput")
    masklo_d = dr("masklo2", [64, 512], BF16, "ExternalInput")

    out_fm = dr("out_fm", [D, T], F32, "ExternalOutput")
    x1_dram = dr("x1s", [D, T], F32, "Internal")
    w_dram = dr("ws_gates", [16, T], BF16, "Internal")
    w1c = dr("w1c", [NPE, 128, NCC * HE], BF16, "Internal")
    w2c = dr("w2c", [NPE, 128, NCC * HE], BF16, "Internal")
    w3c = dr("w3c", [NPE, 128, NHC * 1024], BF16, "Internal")
    ln2_dram = dr("ln2s", [D, T], BF16, "Internal")
    if debug:
        x1_dbg = dr("x1_dbg", [D, T], F32, "ExternalOutput")
        ln2_dbg = dr("ln2_dbg", [D, T], BF16, "ExternalOutput")
        w_dbg = dr("w_dbg", [16, T], BF16, "ExternalOutput")

    ctx = ExitStack()
    with ctx:
        b = Build(nc, ctx)
        sb = lambda n, sh, dt: ctx.enter_context(nc.sbuf_tensor(n, sh, dt))
        psum = lambda n, sh: ctx.enter_context(nc.psum_tensor(n, sh, F32))

        b.make_lanes({"sync": 4, "gpsimd": 4, "scalar": 4})
        pe = b.sem("pe")
        ve = b.sem("ve")
        se = b.sem("se")
        po = b.sem("po")

        wg32 = sb("wg32", [128, NCC * NE], F32)
        gamma = sb("gamma", [128, NCC], F32)
        permP = sb("permP_sb", [128, 128], BF16)
        identB = sb("identB_sb", [128, 128], BF16)
        selAB = sb("selAB_sb", [2, 256], BF16)
        selW = sb("selW_sb", [16, NE * 128], BF16)
        maskhi = sb("maskhi_sb", [128, 512], BF16)
        masklo = sb("masklo_sb", [64, 512], BF16)
        onesc = sb("onesc", [128, 1], BF16)
        ones1f = sb("ones1f", [1, 128], F32)
        cs_row = sb("cs_row", [1, 8], F32)
        cs_rep = sb("cs_rep", [128, 8], F32)
        ab_tm = sb("ab_tm", [128, 8], F32)
        onesc64 = sb("onesc64", [128, 1], BF16)
        ones1 = sb("ones1", [1, 128], BF16)
        onef = sb("onef", [128, 1], F32)
        epsln = sb("epsln", [1, 1], F32)
        epsrms = sb("epsrms", [1, 1], F32)
        w_fmt = sb("w_fmt", [16, 512], BF16)

        B = [psum(f"B{i}", [128, 512]) for i in range(8)]

        stA = ExitStack()
        sbA = lambda n, sh, dt: stA.enter_context(nc.sbuf_tensor(n, sh, dt))
        wq_bf = sbA("wq_bf", [128, NCC * 1024], BF16)
        wk_bf = sbA("wk_bf", [128, NCC * 1024], BF16)
        wv_bf = sbA("wv_bf", [128, NCC * 1024], BF16)
        wo_bf = sbA("wo_bf", [128, NCC * 1024], BF16)
        x_sb2 = sbA("x_sb2", [128, 2 * NCC * 512], F32)
        x_sb = x_sb2
        ln1_32 = sbA("ln1_32", [128, NCC * 512], F32)
        arena_bf = sbA("arena_bf", [128, 8192], BF16)
        arena32 = arena_bf[:].bitcast(F32)
        ln1_bf = sbA("ln1_bf", [128, NCC * 512], BF16)
        qA = sbA("qA", [64, NCC * 512], BF16)
        qB = sbA("qB", [64, NCC * 512], BF16)
        kA = sbA("kA", [64, NCC * 512], BF16)
        kB = sbA("kB", [64, NCC * 512], BF16)
        kA_halo = sbA("kA_halo", [64, NCC * 64], BF16)
        kB_halo = sbA("kB_halo", [64, NCC * 64], BF16)
        msg_fm = ln1_bf
        v_tm = sbA("v_tm", [128, 4 * 1024], BF16)
        v_lo = sbA("v_lo", [128, 1024], BF16)
        tau_hi = sbA("tau_hi", [128, 16 * 128], BF16)
        tau_lo = sbA("tau_lo", [128, 16 * 128], BF16)
        msg_n = sbA("msg_n", [128, 1024], BF16)
        q_raw = tau_hi
        t1 = msg_n
        t2 = tau_lo
        ropeC = sbA("ropeC", [128, 512], BF16)
        ropeS = sbA("ropeS", [128, 512], BF16)
        rows32 = arena32[0:1, 2048:3584]
        rows_bf_t = sbA("rows_bf", [1, 1024], BF16)
        rows_bf = rows_bf_t[0:1, :]
        mass_sb = sbA("mass_sb", [128, 16], F32)
        recip_sb = sbA("recip_sb", [128, 16], F32)
        r_sb = sbA("r_sb", [128, 32], F32)
        srt_sb = sbA("srt_sb", [128, 32], F32)
        gd_sb = sbA("gd_sb", [128, 4], F32)
        p2_sb = sbA("p2_sb", [128, 4], F32)
        pd_sb = sbA("pd_sb", [128, 4], F32)
        wtmp1 = sbA("wtmp1", [128, 32], F32)
        wtmp2 = sbA("wtmp2", [128, 32], F32)
        w32 = sbA("w32", [128, 32], F32)
        wstack = sbA("wstack", [128, 64], BF16)

        b.dma("sync", gamma[:], gamma_d)
        b.dma("sync", wg32[:].rearrange("p (c e) -> p c e", c=NCC),
              wg_d.rearrange("(c p) e -> p c e", p=128))
        b.dma("sync", permP[:], perm_d)
        b.dma("sync", identB[:], ident_d)
        b.dma("sync", selAB[:], selab_d)
        b.dma("sync", selW[:], selw_d)
        b.dma("sync", maskhi[:], maskhi_d)
        b.dma("sync", masklo[:], masklo_d)
        b.op("vector", lambda e: e.memset(onesc[:], 1.0))
        b.op("vector", lambda e: e.memset(ones1f[:], 1.0))
        b.op("vector", lambda e: e.memset(onesc64[:], 0.0))
        b.drain("vector")
        b.op("vector", lambda e: e.memset(onesc64[0:64, :], 1.0))
        b.op("vector", lambda e: e.memset(ones1[:], 1.0))
        b.op("vector", lambda e: e.memset(onef[:], 1.0))
        b.op("vector", lambda e: e.memset(epsln[:], 1e-5))
        b.op("vector", lambda e: e.memset(epsrms[:], 1e-6))
        b.op("vector", lambda e: e.memset(tau_lo[:], 0.0))
        b.op("vector", lambda e: e.memset(v_lo[:], 0.0), inc=ve)
        ve_memset = ve.n

        cast_prev = 0
        for wi, (wdram, wsb) in enumerate([(wq_d, wq_bf), (wk_d, wk_bf), (wv_d, wv_bf), (wo_d, wo_bf)]):
            for half in range(2):
                for j in range(4):
                    cc = half * 4 + j
                    t_ = b.dma("sync", x_sb[:, j * 1024:(j + 1) * 1024],
                               wdram[cc * 128:(cc + 1) * 128, :],
                               waits=[(se, cast_prev)] if j == 0 else ())
                    b.op("scalar", lambda e, j=j, cc=cc, wsb=wsb:
                         e.copy(wsb[:, cc * 1024:(cc + 1) * 1024], x_sb[:, j * 1024:(j + 1) * 1024]),
                         waits=[t_], inc=se)
                cast_prev = se.n
        se_setup = se.n
        qw_setup0 = b.lane_snap("sync")
        b.waits("vector", qw_setup0)
        for cc in range(NCC):
            b.op("tensor", lambda e, cc=cc:
                 e.matmul(B[6][0:1, 0:8], onef[:], wg32[:, cc * 8:(cc + 1) * 8],
                          start=(cc == 0), stop=(cc == NCC - 1)),
                 waits=(qw_setup0 + [(ve, ve.n)]) if cc == 0 else (),
                 inc=pe if cc == NCC - 1 else None)
        b.op("vector", lambda e: e.tensor_copy(cs_row[:], B[6][0:1, 0:8]), waits=[(pe, pe.n)], inc=ve)
        b.op("tensor", lambda e: e.matmul(B[6][:, 0:8], ones1f[:], cs_row[:], start=True, stop=True),
             waits=[(ve, ve.n)], inc=pe)
        b.op("vector", lambda e: e.tensor_copy(cs_rep[:], B[6][:, 0:8]), waits=[(pe, pe.n)], inc=ve)
        ve_csrep = ve.n
        qw_setup = b.lane_snap("sync")
        b.waits("tensor", qw_setup + [(ve, ve_memset)])
        b.waits("vector", qw_setup)

        xsb_free = {0: (ve, 0), 1: (ve, 0)}
        ln132_free = []
        ln1bf_free = []
        qbf_free = (pe, 0)
        kbf_free = (pe, 0)
        rope_free = [(ve, 0)]
        kv_halo_ready = (ve, ve_memset)
        bank_free = {i: (pe, 0) for i in range(8)}
        vtm_free = (pe, 0)
        stores = []
        w_stores = []
        w_fmt_free = []
        x1_store_q = []
        ve_ln2_prev = 0
        pe_st2_prev = 0
        ve_x1_prev = 0

        for i in range(n_tiles):
            t0 = i * TILE
            xo = (i % 2) * NCC * 512
            qx_x = b.dma("gpsimd", x_sb2[:, xo:xo + NCC * 512].rearrange("p (c t) -> p c t", c=NCC),
                         x_fm[:, t0:t0 + TILE].rearrange("(c p) t -> p c t", p=128),
                         waits=[xsb_free[i % 2], (se, se_setup if i == 0 else 0)])
            rp = 0
            ct_t = b.dma("sync", ropeC[:, 0:512], cos_d[:, t0:t0 + TILE],
                         waits=[rope_free[0]])
            st_t = b.dma("sync", ropeS[:, 0:512], sin_d[:, t0:t0 + TILE])

            se_stats0 = se.n
            for cc in range(NCC):
                b.op("scalar", lambda e, cc=cc, xo=xo:
                     e.copy(arena_bf[:, cc * 512:(cc + 1) * 512], x_sb2[:, xo + cc * 512:xo + (cc + 1) * 512]),
                     waits=[qx_x, (pe, pe_st2_prev), (ve, ve_x1_prev)] if cc == 0 else ())
                b.op("scalar", lambda e, cc=cc, xo=xo:
                     e.activation(arena_bf[:, 4096 + cc * 512:4096 + (cc + 1) * 512],
                                  x_sb2[:, xo + cc * 512:xo + (cc + 1) * 512], AF.Square),
                     inc=se)
            se_stats = se.n

            def ln_stats():
                for cc in range(NCC):
                    b.op("tensor", lambda e, cc=cc:
                         e.matmul(B[4][0:1, :], onesc[:], arena_bf[:, cc * 512:(cc + 1) * 512],
                                  start=(cc == 0), stop=(cc == NCC - 1)),
                         waits=([bank_free[4], bank_free[5], (se, min(se_stats0 + cc + 1, se_stats))]
                                if cc == 0 else [(se, min(se_stats0 + cc + 1, se_stats))]))
                    b.op("tensor", lambda e, cc=cc:
                         e.matmul(B[5][0:1, :], onesc[:], arena_bf[:, 4096 + cc * 512:4096 + (cc + 1) * 512],
                                  start=(cc == 0), stop=(cc == NCC - 1)),
                         inc=pe if cc == NCC - 1 else None)
                return pe.n

            def ln_rowops(pe_stats_v):
                R0 = rows32[0:1, 0:512]
                R1 = rows32[0:1, 512:1024]
                R2 = rows32[0:1, 1024:1536]
                b.op("vector", lambda e: e.tensor_scalar(R0, B[4][0:1, :], 1.0 / D, None, ALU.mult),
                     waits=[(pe, pe_stats_v)])
                b.drain("vector")
                b.op("vector", lambda e: e.tensor_tensor(R1, R0, R0, ALU.mult))
                b.drain("vector")
                b.op("vector", lambda e: e.scalar_tensor_tensor(R1, B[5][0:1, :], 1.0 / D,
                                                                R1, ALU.mult, ALU.subtract),
                     inc=ve)
                b.op("scalar", lambda e: e.activation(R1, R1, AF.Ln, bias=epsln[0:1, :]),
                     waits=[(ve, ve.n)])
                b.drain("scalar")
                b.op("scalar", lambda e: e.activation(R2, R1, AF.Exp, scale=-0.5), inc=se)
                b.op("vector", lambda e: e.tensor_copy(rows_bf[0:1, 0:512], R2), waits=[(se, se.n)])
                b.op("vector", lambda e: e.scalar_tensor_tensor(R1, R0, -1.0, R2, ALU.mult, ALU.mult))
                b.drain("vector")
                b.op("vector", lambda e: e.tensor_copy(rows_bf[0:1, 512:1024], R1), inc=ve)
                return ve.n

            pe_st = ln_stats()
            ve_rows = ln_rowops(pe_st)
            b.op("tensor", lambda e: e.matmul(B[4][:], ones1[:], rows_bf[0:1, 0:512], start=True, stop=True),
                 waits=[(ve, ve_rows)])
            b.op("tensor", lambda e: e.matmul(B[5][:], ones1[:], rows_bf[0:1, 512:1024], start=True, stop=True),
                 inc=pe)
            pe_bc = pe.n
            for cc in range(NCC):
                b.op("vector", lambda e, cc=cc, xo=xo:
                     e.tensor_tensor(ln1_32[:, cc * 512:(cc + 1) * 512],
                                     x_sb2[:, xo + cc * 512:xo + (cc + 1) * 512],
                                     B[4][:], ALU.mult),
                     waits=([(pe, pe_bc)] + ln132_free) if cc == 0 else ())
            b.drain("vector")
            ve_ln1_base = ve.n
            for cc in range(NCC):
                b.op("vector", lambda e, cc=cc:
                     e.tensor_tensor(ln1_32[:, cc * 512:(cc + 1) * 512], ln1_32[:, cc * 512:(cc + 1) * 512],
                                     B[5][:], ALU.add),
                     inc=ve)
            ve_ln1 = ve.n
            bank_free[4] = (ve, ve_ln1)
            bank_free[5] = (ve, ve_ln1)
            se_ln1_base = se.n
            for cc in range(NCC):
                b.op("scalar", lambda e, cc=cc:
                     e.copy(ln1_bf[:, cc * 512:(cc + 1) * 512], ln1_32[:, cc * 512:(cc + 1) * 512]),
                     waits=([(ve, ve_ln1_base + cc + 1)] + (ln1bf_free if cc == 0 else [])),
                     inc=se)
            se_ln1 = se.n

            if stages < 2:
                continue
            keys = [(ti, fc) for ti in range(2) for fc in range(NCC)]
            se_evac = {}
            pe_perm = {}
            ve_rope = {}

            def emit_projection(kidx):
                ti, fc = keys[kidx]
                wsb = wq_bf if ti == 0 else wk_bf
                bp = B[kidx % 2]
                for cc in range(NCC):
                    wts = []
                    if kidx == 0:
                        wts = [(se, se_ln1_base + cc + 1)]
                        if cc == 0:
                            wts += [bank_free[kidx % 2], qbf_free, kbf_free]
                    elif cc == 0:
                        wts = [(se, se_ln1), bank_free[kidx % 2]]
                    b.op("tensor", lambda e, cc=cc, fc=fc, wsb=wsb, bp=bp:
                         e.matmul(bp[:], wsb[:, cc * 1024 + fc * 128: cc * 1024 + fc * 128 + 128],
                                  ln1_bf[:, cc * 512:(cc + 1) * 512],
                                  start=(cc == 0), stop=(cc == NCC - 1)),
                         waits=wts,
                         inc=pe if cc == NCC - 1 else None)
                pe_proj = pe.n
                wts2 = [(pe, pe_proj)]
                if kidx >= 2:
                    wts2.append(ve_rope[kidx - 2])
                b.op("scalar", lambda e, kidx=kidx, bp=bp:
                     e.copy(q_raw[:, 1024 + (kidx % 2) * 512:1024 + ((kidx % 2) + 1) * 512], bp[:]),
                     waits=wts2, inc=se)
                se_evac[kidx] = (se, se.n)
                bank_free[kidx % 2] = (se, se.n)
                b.op("tensor", lambda e, kidx=kidx:
                     e.matmul(B[2 + kidx % 2][:], permP[:],
                              q_raw[:, 1024 + (kidx % 2) * 512:1024 + ((kidx % 2) + 1) * 512], start=True, stop=True),
                     waits=[se_evac[kidx], bank_free[2 + kidx % 2]], inc=pe)
                pe_perm[kidx] = (pe, pe.n)

            for kp in range(0, 16, 2):
                emit_projection(kp)
                emit_projection(kp + 1)
                for kidx in (kp, kp + 1):
                    ti, fc = keys[kidx]
                    b.op("vector", lambda e, kidx=kidx, rp=rp:
                         e.tensor_tensor(t1[:, (kidx % 2) * 512:((kidx % 2) + 1) * 512],
                                         q_raw[:, 1024 + (kidx % 2) * 512:1024 + ((kidx % 2) + 1) * 512],
                                         ropeC[:, rp * 512:(rp + 1) * 512], ALU.mult),
                         waits=[se_evac[kidx], ct_t])
                    b.op("vector", lambda e, kidx=kidx, rp=rp:
                         e.tensor_tensor(t2[:, (kidx % 2) * 512:((kidx % 2) + 1) * 512],
                                         B[2 + kidx % 2][:],
                                         ropeS[:, rp * 512:(rp + 1) * 512], ALU.mult),
                         waits=[pe_perm[kidx], st_t])
                b.drain("vector")
                for kidx in (kp, kp + 1):
                    ti, fc = keys[kidx]
                    dstA = qA if ti == 0 else kA
                    dstB = qB if ti == 0 else kB
                    wts = []
                    if kidx == 0:
                        wts = [qbf_free]
                    elif kidx == NCC:
                        wts = [kbf_free]
                    b.op("vector", lambda e, fc=fc, kidx=kidx, dstA=dstA:
                         e.tensor_tensor(dstA[0:64, fc * 512:(fc + 1) * 512],
                                         t1[0:64, (kidx % 2) * 512:((kidx % 2) + 1) * 512],
                                         t2[0:64, (kidx % 2) * 512:((kidx % 2) + 1) * 512], ALU.add),
                         waits=wts)
                    b.op("vector", lambda e, fc=fc, kidx=kidx, dstB=dstB:
                         e.tensor_tensor(dstB[0:64, fc * 512:(fc + 1) * 512],
                                         t1[64:128, (kidx % 2) * 512:((kidx % 2) + 1) * 512],
                                         t2[64:128, (kidx % 2) * 512:((kidx % 2) + 1) * 512], ALU.add),
                         inc=ve)
                    ve_rope[kidx] = (ve, ve.n)
                    bank_free[2 + kidx % 2] = (ve, ve.n)
            ve_rope_all = ve.n
            rope_free[0] = (ve, ve_rope_all)

            ve_v = {}
            for tb2 in range(4):
                for half in range(2):
                    bk = (tb2 * 2 + half) % 4
                    for cc in range(NCC):
                        b.op("tensor", lambda e, cc=cc, tb2=tb2, half=half, bk=bk:
                             e.matmul(B[bk][:],
                                      ln1_bf[:, cc * 512 + tb2 * 128: cc * 512 + tb2 * 128 + 128],
                                      wv_bf[:, cc * 1024 + half * 512: cc * 1024 + half * 512 + 512],
                                      start=(cc == 0), stop=(cc == NCC - 1)),
                             waits=[bank_free[bk], vtm_free] if cc == 0 else (),
                             inc=pe if cc == NCC - 1 else None)
                    pe_vp = pe.n
                    b.op("vector", lambda e, tb2=tb2, half=half, bk=bk:
                         e.tensor_copy(v_tm[:, tb2 * 1024 + half * 512: tb2 * 1024 + half * 512 + 512],
                                       B[bk][:]),
                         waits=[(pe, pe_vp)], inc=ve)
                    bank_free[bk] = (ve, ve.n)
                ve_v[tb2] = (ve, ve.n)

            if stages < 3:
                continue
            pe_msg_prev = 0
            pe_msg_all = 0
            se_msgn_prev = None
            for qb in range(4):
                lo_valid = not (i == 0 and qb == 0)
                if qb >= 1:
                    b.op("vector", lambda e, qb=qb:
                         e.tensor_copy(v_lo[0:64, :], v_tm[64:128, (qb - 1) * 1024: qb * 1024]),
                         waits=[ve_v[qb - 1], (pe, pe_msg_prev)], inc=ve)
                ve_vlo = ve.n

                mass_pending = []
                for burst in range(2):
                    for hj in range(8):
                        h = 8 * burst + hj
                        fc = h // 2
                        hibank = B[2 + hj // 4]
                        lobank = B[4 + hj // 4]
                        col = (hj % 4) * 128
                        ksrc = kA if h % 2 == 0 else kB
                        qsrc = qA if h % 2 == 0 else qB
                        khalo = kA_halo if h % 2 == 0 else kB_halo
                        wts = []
                        if hj == 0 or hj == 4:
                            wts = [bank_free[2 + hj // 4], bank_free[4 + hj // 4]]
                            if qb == 0 and burst == 0 and hj == 0:
                                wts += [(ve, ve_rope_all), kv_halo_ready]
                        b.op("tensor", lambda e, fc=fc, qb=qb, hibank=hibank, col=col, ksrc=ksrc, qsrc=qsrc:
                             e.matmul(hibank[:, col:col + 128],
                                      ksrc[0:64, fc * 512 + qb * 128: fc * 512 + qb * 128 + 128],
                                      qsrc[0:64, fc * 512 + qb * 128: fc * 512 + qb * 128 + 128],
                                      start=True, stop=True),
                             waits=wts,
                             inc=pe if (hj == 7 and not lo_valid) else None)
                        if lo_valid:
                            if qb == 0:
                                ks = khalo[0:64, fc * 64:(fc + 1) * 64]
                            else:
                                ks = ksrc[0:64, fc * 512 + qb * 128 - 64: fc * 512 + qb * 128]
                            b.op("tensor", lambda e, lobank=lobank, col=col, ks=ks, fc=fc, qb=qb, qsrc=qsrc:
                                 e.matmul(lobank[0:64, col:col + 128], ks,
                                          qsrc[0:64, fc * 512 + qb * 128: fc * 512 + qb * 128 + 128],
                                          start=True, stop=True),
                                 inc=pe if hj == 7 else None)
                    pe_sc = pe.n

                    if burst == 0 and se_msgn_prev is not None:
                        sm, pqb = se_msgn_prev
                        for fc in range(NCC):
                            b.op("tensor", lambda e, fc=fc:
                                 e.transpose(B[1][:, fc * 64:fc * 64 + 64].bitcast(BF16),
                                             msg_n[:, fc * 128:(fc + 1) * 128], identB[:]),
                                 waits=[(se, sm), bank_free[1]] if fc == 0 else (),
                                 inc=pe if fc == NCC - 1 else None)
                        pe_tr = pe.n
                        for fc in range(NCC):
                            b.op("vector", lambda e, fc=fc, pqb=pqb:
                                 e.tensor_copy(msg_fm[:, fc * 512 + pqb * 128: fc * 512 + pqb * 128 + 128],
                                               B[1][:, fc * 64:fc * 64 + 64].bitcast(BF16)),
                                 waits=[(pe, pe_tr)] if fc == 0 else (),
                                 inc=ve if fc == NCC - 1 else None)
                        bank_free[1] = (ve, ve.n)
                        se_msgn_prev = None

                    sig_half = {}
                    for half in range(2):
                        hb = B[2 + half]
                        lb = B[4 + half]
                        b.op("scalar", lambda e, burst=burst, half=half, hb=hb:
                             e.activation(tau_hi[:, burst * 1024 + half * 512: burst * 1024 + (half + 1) * 512],
                                          hb[:], AF.Sigmoid, scale=0.125),
                             waits=[(pe, pe_sc)] if half == 0 else (),
                             inc=None if lo_valid else se)
                        if lo_valid:
                            b.op("scalar", lambda e, burst=burst, half=half, lb=lb:
                                 e.activation(tau_lo[0:64, burst * 1024 + half * 512: burst * 1024 + (half + 1) * 512],
                                              lb[0:64, :], AF.Sigmoid, scale=0.125),
                                 inc=se)
                        sig_half[half] = (se, se.n)
                        bank_free[2 + half] = (se, se.n)
                        bank_free[4 + half] = (se, se.n)
                    mask_half = {}
                    for half in range(2):
                        b.op("vector", lambda e, burst=burst, half=half:
                             e.tensor_tensor(tau_hi[:, burst * 1024 + half * 512: burst * 1024 + (half + 1) * 512],
                                             tau_hi[:, burst * 1024 + half * 512: burst * 1024 + (half + 1) * 512],
                                             maskhi[:], ALU.mult),
                             waits=[sig_half[half]],
                             inc=None if lo_valid else ve)
                        if lo_valid:
                            b.op("vector", lambda e, burst=burst, half=half:
                                 e.tensor_tensor(tau_lo[0:64, burst * 1024 + half * 512: burst * 1024 + (half + 1) * 512],
                                                 tau_lo[0:64, burst * 1024 + half * 512: burst * 1024 + (half + 1) * 512],
                                                 masklo[:], ALU.mult),
                                 inc=ve)
                        mask_half[half] = (ve, ve.n)

                    for mi, (hm, lo_m) in enumerate(mass_pending):
                        mwts = [bank_free[0]] if mi == 0 else []
                        if lo_m:
                            b.op("tensor", lambda e, h=hm:
                                 e.matmul(B[0][:, h:h + 1], tau_lo[:, h * 128:(h + 1) * 128], onesc64[:],
                                          start=True, stop=False),
                                 waits=mwts)
                            mwts = []
                        b.op("tensor", lambda e, h=hm, lo=lo_m:
                             e.matmul(B[0][:, h:h + 1], tau_hi[:, h * 128:(h + 1) * 128], onesc[:],
                                      start=(not lo), stop=True),
                             waits=mwts)
                    mass_pending = []

                    for hj in range(8):
                        h = 8 * burst + hj
                        mb = B[6 + h // 8]
                        wts = []
                        if hj == 0:
                            wts = [mask_half[0], bank_free[6 + burst]]
                            if burst == 0:
                                wts += [(ve, ve_vlo)]
                        elif hj == 4:
                            wts = [mask_half[1]]
                        if lo_valid:
                            b.op("tensor", lambda e, h=h, mb=mb:
                                 e.matmul(mb[:, (h % 8) * 64:((h % 8) + 1) * 64],
                                          tau_lo[:, h * 128:(h + 1) * 128], v_lo[:, h * 64:(h + 1) * 64],
                                          start=True, stop=False),
                                 waits=wts)
                            wts = []
                        b.op("tensor", lambda e, h=h, mb=mb, qb=qb, lo=lo_valid:
                             e.matmul(mb[:, (h % 8) * 64:((h % 8) + 1) * 64],
                                      tau_hi[:, h * 128:(h + 1) * 128],
                                      v_tm[:, qb * 1024 + h * 64: qb * 1024 + h * 64 + 64],
                                      start=(not lo), stop=True),
                             waits=wts)
                        mass_pending.append((h, lo_valid))
                for mi, (hm, lo_m) in enumerate(mass_pending):
                    mwts = [bank_free[0]] if (mi == 0 and hm < 8) else []
                    if lo_m:
                        b.op("tensor", lambda e, h=hm:
                             e.matmul(B[0][:, h:h + 1], tau_lo[:, h * 128:(h + 1) * 128], onesc64[:],
                                      start=True, stop=False),
                             waits=mwts)
                        mwts = []
                    b.op("tensor", lambda e, h=hm, lo=lo_m:
                         e.matmul(B[0][:, h:h + 1], tau_hi[:, h * 128:(h + 1) * 128], onesc[:],
                                  start=(not lo), stop=True),
                         waits=mwts,
                         inc=pe if mi == len(mass_pending) - 1 else None)
                mass_pending = []
                pe_msg = pe.n
                pe_msg_prev = pe_msg
                pe_msg_all = pe_msg

                b.op("vector", lambda e: e.tensor_scalar_max(mass_sb[:], B[0][:, 0:16], 1e-6),
                     waits=[(pe, pe_msg)])
                b.drain("vector")
                b.op("vector", lambda e: e.reciprocal(recip_sb[:], mass_sb[:]), inc=ve)
                ve_recip = ve.n
                bank_free[0] = (ve, ve_recip)
                for h in range(NH):
                    b.op("scalar", lambda e, h=h:
                         e.activation(msg_n[:, h * 64:(h + 1) * 64],
                                      B[6 + h // 8][:, (h % 8) * 64:((h % 8) + 1) * 64],
                                      AF.Copy, scale=recip_sb[:, h:h + 1]),
                         waits=[(ve, ve_recip)] if h == 0 else (),
                         inc=se if h == NH - 1 else None)
                se_msgn = se.n
                bank_free[6] = (se, se_msgn)
                bank_free[7] = (se, se_msgn)
                se_msgn_prev = (se_msgn, qb)
            sm, pqb = se_msgn_prev
            se_msgn_prev = None
            for fc in range(NCC):
                b.op("tensor", lambda e, fc=fc:
                     e.transpose(B[1][:, fc * 64:fc * 64 + 64].bitcast(BF16),
                                 msg_n[:, fc * 128:(fc + 1) * 128], identB[:]),
                     waits=[(se, sm), bank_free[1]] if fc == 0 else (),
                     inc=pe if fc == NCC - 1 else None)
            pe_tr = pe.n
            for fc in range(NCC):
                b.op("vector", lambda e, fc=fc, pqb=pqb:
                     e.tensor_copy(msg_fm[:, fc * 512 + pqb * 128: fc * 512 + pqb * 128 + 128],
                                   B[1][:, fc * 64:fc * 64 + 64].bitcast(BF16)),
                     waits=[(pe, pe_tr)] if fc == 0 else (),
                     inc=ve if fc == NCC - 1 else None)
            bank_free[1] = (ve, ve.n)
            ve_msgfm = ve.n
            vtm_free = (pe, pe_msg_all)

            b.op("vector", lambda e:
                 e.tensor_copy(kA_halo[0:64].rearrange("p (c t) -> p c t", c=NCC),
                               kA[0:64].rearrange("p (c t) -> p c t", c=NCC)[:, :, 448:512]),
                 waits=[(pe, pe_msg_all)])
            b.op("vector", lambda e:
                 e.tensor_copy(kB_halo[0:64].rearrange("p (c t) -> p c t", c=NCC),
                               kB[0:64].rearrange("p (c t) -> p c t", c=NCC)[:, :, 448:512]))
            b.op("vector", lambda e: e.tensor_copy(v_lo[0:64, :], v_tm[64:128, 3 * 1024:4 * 1024]),
                 inc=ve)
            kv_halo_ready = (ve, ve.n)
            kbf_free = (ve, ve.n)

            if stages < 4:
                continue
            pe_wo_last = 0
            for fc in range(NCC):
                bk = fc % 2
                for cc in range(NCC):
                    b.op("tensor", lambda e, cc=cc, fc=fc, bk=bk:
                         e.matmul(B[bk][:], wo_bf[:, cc * 1024 + fc * 128: cc * 1024 + fc * 128 + 128],
                                  msg_fm[:, cc * 512:(cc + 1) * 512],
                                  start=(cc == 0), stop=(cc == NCC - 1)),
                         waits=[(ve, ve_msgfm), bank_free[bk]] if cc == 0 else (),
                         inc=pe if cc == NCC - 1 else None)
                pe_wo = pe.n
                pe_wo_last = pe_wo
                b.op("vector", lambda e, fc=fc, bk=bk:
                     e.tensor_tensor(ln1_32[:, fc * 512:(fc + 1) * 512], B[bk][:],
                                     ln1_32[:, fc * 512:(fc + 1) * 512], ALU.add),
                     waits=[(pe, pe_wo)], inc=ve)
                bank_free[bk] = (ve, ve.n)
            ve_pre = ve.n
            qbf_free = (pe, pe_wo_last)

            for cc in range(NCC):
                b.op("scalar", lambda e, cc=cc:
                     e.activation(arena_bf[:, cc * 512:(cc + 1) * 512], ln1_32[:, cc * 512:(cc + 1) * 512],
                                  AF.Square),
                     waits=[(ve, ve_pre)] if cc == 0 else (),
                     inc=se if cc == NCC - 1 else None)
            se_rsq = se.n
            for cc in range(NCC):
                b.op("tensor", lambda e, cc=cc:
                     e.matmul(B[4][0:1, :], onesc[:], arena_bf[:, cc * 512:(cc + 1) * 512],
                              start=(cc == 0), stop=(cc == NCC - 1)),
                     waits=[(se, se_rsq), bank_free[4]] if cc == 0 else (),
                     inc=pe if cc == NCC - 1 else None)
            pe_rstat = pe.n
            b.op("vector", lambda e: e.tensor_scalar(rows32[0:1, 0:512], B[4][0:1, :], 1.0 / D, None, ALU.mult),
                 waits=[(pe, pe_rstat)], inc=ve)
            b.op("scalar", lambda e: e.activation(rows32[0:1, 0:512], rows32[0:1, 0:512], AF.Ln, bias=epsrms[0:1, :]),
                 waits=[(ve, ve.n)])
            b.drain("scalar")
            b.op("scalar", lambda e: e.activation(rows32[0:1, 1024:1536], rows32[0:1, 0:512], AF.Exp, scale=-0.5),
                 inc=se)
            b.op("vector", lambda e: e.tensor_copy(rows_bf[0:1, 0:512], rows32[0:1, 1024:1536]),
                 waits=[(se, se.n)], inc=ve)
            ve_rr = ve.n
            b.op("tensor", lambda e: e.matmul(B[4][:], ones1[:], rows_bf[0:1, 0:512], start=True, stop=True),
                 waits=[(ve, ve_rr)], inc=pe)
            pe_rbc = pe.n
            for cc in range(NCC):
                b.op("vector", lambda e, cc=cc:
                     e.tensor_tensor(arena32[:, cc * 512:(cc + 1) * 512], ln1_32[:, cc * 512:(cc + 1) * 512],
                                     B[4][:], ALU.mult),
                     waits=[(pe, pe_rbc)] if cc == 0 else ())
            b.drain("vector")
            for cc in range(NCC):
                b.op("vector", lambda e, cc=cc, xo=xo:
                     e.scalar_tensor_tensor(ln1_32[:, cc * 512:(cc + 1) * 512],
                                            arena32[:, cc * 512:(cc + 1) * 512],
                                            gamma[:, cc:cc + 1],
                                            x_sb2[:, xo + cc * 512:xo + (cc + 1) * 512],
                                            ALU.mult, ALU.add),
                     inc=ve if cc == NCC - 1 else None)
            ve_x1 = ve.n
            ve_x1_prev = ve_x1
            xsb_free[i % 2] = (ve, ve_x1)
            bank_free[4] = (ve, ve_x1)

            x1_ts = b.dma_multi("scalar",
                                [(x1_dram[cc * 128:(cc + 1) * 128, t0:t0 + TILE],
                                  ln1_32[:, cc * 512:(cc + 1) * 512]) for cc in range(NCC)],
                                waits=[(ve, ve_x1)])
            x1_store_q.append(x1_ts)

            if stages < 5:
                continue
            se_stats0 = se.n
            for cc in range(NCC):
                b.op("scalar", lambda e, cc=cc:
                     e.copy(arena_bf[:, cc * 512:(cc + 1) * 512], ln1_32[:, cc * 512:(cc + 1) * 512]),
                     waits=[(ve, ve_x1)] if cc == 0 else ())
                b.op("scalar", lambda e, cc=cc:
                     e.activation(arena_bf[:, 4096 + cc * 512:4096 + (cc + 1) * 512],
                                  ln1_32[:, cc * 512:(cc + 1) * 512], AF.Square),
                     inc=se)
            se_stats = se.n
            pe_st2 = ln_stats()
            pe_st2_prev = pe_st2
            ve_rows2 = ln_rowops(pe_st2)
            b.op("tensor", lambda e: e.matmul(B[4][:], ones1[:], rows_bf[0:1, 0:512], start=True, stop=True),
                 waits=[(ve, ve_rows2)])
            b.op("tensor", lambda e: e.matmul(B[5][:], ones1[:], rows_bf[0:1, 512:1024], start=True, stop=True),
                 inc=pe)
            pe_bc2 = pe.n
            for cc in range(NCC):
                b.op("vector", lambda e, cc=cc:
                     e.tensor_tensor(ln1_bf[:, cc * 512:(cc + 1) * 512], ln1_32[:, cc * 512:(cc + 1) * 512],
                                     B[4][:], ALU.mult),
                     waits=([(pe, pe_bc2)] + ln1bf_free) if cc == 0 else ())
            b.drain("vector")
            for cc in range(NCC):
                b.op("vector", lambda e, cc=cc:
                     e.tensor_tensor(ln1_bf[:, cc * 512:(cc + 1) * 512], ln1_bf[:, cc * 512:(cc + 1) * 512],
                                     B[5][:], ALU.add),
                     inc=ve if cc == NCC - 1 else None)
            ve_ln2 = ve.n
            ve_ln2_prev = ve_ln2
            bank_free[4] = (ve, ve_ln2)
            bank_free[5] = (ve, ve_ln2)
            l2_ts = b.dma_multi("scalar",
                                [(ln2_dram[cc * 128:(cc + 1) * 128, t0:t0 + TILE],
                                  ln1_bf[:, cc * 512:(cc + 1) * 512]) for cc in range(NCC)],
                                waits=[(ve, ve_ln2)])

            for tb2 in range(4):
                for cc in range(NCC):
                    b.op("tensor", lambda e, cc=cc, tb2=tb2:
                         e.matmul(B[6][:, tb2 * 8:(tb2 + 1) * 8],
                                  ln1_32[:, cc * 512 + tb2 * 128: cc * 512 + tb2 * 128 + 128],
                                  wg32[:, cc * 8:(cc + 1) * 8],
                                  start=(cc == 0), stop=(cc == NCC - 1)),
                         waits=[(ve, ve_x1), bank_free[6]] if (tb2 == 0 and cc == 0) else (),
                         inc=pe if (tb2 == 3 and cc == NCC - 1) else None)
            pe_rt = pe.n
            for j in range(8):
                b.op("tensor", lambda e, j=j:
                     e.transpose(B[7][:, j:j + 1].bitcast(BF16)[:, 0:1],
                                 rows_bf[0:1, j * 128:(j + 1) * 128], identB[0:1, 0:1]),
                     waits=[(ve, ve_rows2), bank_free[7]] if j == 0 else (),
                     inc=pe if j == 7 else None)
            pe_ab = pe.n
            b.op("vector", lambda e:
                 e.tensor_copy(ab_tm[:], B[7][:, 0:8].bitcast(BF16)[:, 0:16:2]),
                 waits=[(pe, pe_ab)], inc=ve)
            bank_free[7] = (ve, ve.n)
            b.drain("vector")
            for tb2 in range(4):
                b.op("vector", lambda e, tb2=tb2:
                     e.tensor_scalar(r_sb[:, tb2 * 8:(tb2 + 1) * 8], B[6][:, tb2 * 8:(tb2 + 1) * 8],
                                     ab_tm[:, tb2:tb2 + 1], None, ALU.mult),
                     waits=[(pe, pe_rt)] if tb2 == 0 else ())
            b.drain("vector")
            for tb2 in range(4):
                b.op("vector", lambda e, tb2=tb2:
                     e.scalar_tensor_tensor(r_sb[:, tb2 * 8:(tb2 + 1) * 8], cs_rep[:],
                                            ab_tm[:, 4 + tb2:5 + tb2],
                                            r_sb[:, tb2 * 8:(tb2 + 1) * 8], ALU.mult, ALU.add),
                     inc=ve if tb2 == 3 else None)
            bank_free[6] = (ve, ve.n)
            b.drain("vector")
            for tb2 in range(4):
                b.op("vector", lambda e, tb2=tb2:
                     e.max(out=srt_sb[:, tb2 * 8:(tb2 + 1) * 8], in_=r_sb[:, tb2 * 8:(tb2 + 1) * 8]))
            b.drain("vector")
            for tb2 in range(4):
                b.op("vector", lambda e, tb2=tb2:
                     e.tensor_tensor(gd_sb[:, tb2:tb2 + 1], srt_sb[:, tb2 * 8 + 1:tb2 * 8 + 2],
                                     srt_sb[:, tb2 * 8:tb2 * 8 + 1], ALU.subtract),
                     inc=ve if tb2 == 3 else None)
            ve_gd = ve.n
            b.op("scalar", lambda e: e.activation(p2_sb[:], gd_sb[:], AF.Sigmoid), waits=[(ve, ve_gd)],
                 inc=se)
            se_gate = se.n
            b.op("vector", lambda e: e.tensor_scalar(pd_sb[:], p2_sb[:], -2.0, 1.0, ALU.mult, ALU.add),
                 waits=[(se, se_gate)])
            b.drain("vector")
            for tb2 in range(4):
                b.op("vector", lambda e, tb2=tb2:
                     e.tensor_scalar(wtmp1[:, tb2 * 8:(tb2 + 1) * 8], r_sb[:, tb2 * 8:(tb2 + 1) * 8],
                                     srt_sb[:, tb2 * 8:tb2 * 8 + 1], None, ALU.is_ge),
                     waits=[(se, se_gate)] if tb2 == 0 else ())
                b.op("vector", lambda e, tb2=tb2:
                     e.tensor_scalar(wtmp2[:, tb2 * 8:(tb2 + 1) * 8], r_sb[:, tb2 * 8:(tb2 + 1) * 8],
                                     srt_sb[:, tb2 * 8 + 1:tb2 * 8 + 2], None, ALU.is_ge))
            b.drain("vector")
            for tb2 in range(4):
                b.op("vector", lambda e, tb2=tb2:
                     e.tensor_scalar(wtmp1[:, tb2 * 8:(tb2 + 1) * 8], wtmp1[:, tb2 * 8:(tb2 + 1) * 8],
                                     pd_sb[:, tb2:tb2 + 1], None, ALU.mult))
            b.drain("vector")
            for tb2 in range(4):
                b.op("vector", lambda e, tb2=tb2:
                     e.scalar_tensor_tensor(w32[:, tb2 * 8:(tb2 + 1) * 8], wtmp2[:, tb2 * 8:(tb2 + 1) * 8],
                                            p2_sb[:, tb2:tb2 + 1], wtmp1[:, tb2 * 8:(tb2 + 1) * 8],
                                            ALU.mult, ALU.add))
            b.drain("vector")
            for tb2 in range(4):
                b.op("vector", lambda e, tb2=tb2:
                     e.tensor_copy(wstack[:, tb2 * 16:tb2 * 16 + 8], w32[:, tb2 * 8:(tb2 + 1) * 8]))
            b.drain("vector")
            for tb2 in range(4):
                b.op("vector", lambda e, tb2=tb2:
                     e.tensor_tensor(wstack[:, tb2 * 16 + 8:tb2 * 16 + 16], w32[:, tb2 * 8:(tb2 + 1) * 8],
                                     wstack[:, tb2 * 16:tb2 * 16 + 8], ALU.subtract),
                     inc=ve if tb2 == 3 else None)
            ve_ws = ve.n
            for tb2 in range(4):
                b.op("tensor", lambda e, tb2=tb2:
                     e.transpose(B[7][0:16, tb2 * 32:tb2 * 32 + 64].bitcast(BF16)[0:16, 0:128]
                                 if False else B[7][0:16, tb2 * 64:tb2 * 64 + 64].bitcast(BF16),
                                 wstack[:, tb2 * 16:(tb2 + 1) * 16], identB[:]),
                     waits=[(ve, ve_ws), bank_free[7]] if tb2 == 0 else (),
                     inc=pe if tb2 == 3 else None)
            pe_wt = pe.n
            for tb2 in range(4):
                b.op("vector", lambda e, tb2=tb2:
                     e.tensor_copy(w_fmt[:, tb2 * 128:(tb2 + 1) * 128],
                                   B[7][0:16, tb2 * 64:tb2 * 64 + 64].bitcast(BF16)),
                     waits=([(pe, pe_wt)] + w_fmt_free) if tb2 == 0 else (),
                     inc=ve if tb2 == 3 else None)
            bank_free[7] = (ve, ve.n)
            w_ts = b.dma_multi("scalar", [(w_dram[:, t0:t0 + TILE], w_fmt[:])], waits=[(ve, ve.n)])
            snap = b.lane_snap("scalar")
            w_fmt_free = snap
            ln1bf_free = snap
            ln132_free = snap
            stores.append(tuple(snap))
            w_stores.append(snap)

        ve_A = ve.n
        se_A = se.n
        pe_A = pe.n
        qs_A = b.lane_snap("scalar")

        if debug:
            b.dma("scalar", x1_dbg, x1_dram, waits=qs_A)
            b.dma("scalar", ln2_dbg, ln2_dram)
            b.dma("scalar", w_dbg, w_dram, waits=w_stores[-1] if w_stores else [])

        stA.close()
        stC = ExitStack()
        ctx.enter_context(stC)
        sbC = lambda n, sh, dt: stC.enter_context(nc.sbuf_tensor(n, sh, dt))
        ln2_q = sbC("ln2_q", [128, 2 * NCC * QT], BF16)
        y_q = sbC("y_q", [128, NCC * QT], F32)
        w1b = sbC("w1b", [128, 2 * NCC * HE], BF16)
        w2b = sbC("w2b", [128, NCC * HE], BF16)
        w3b = sbC("w3b", [128, NHC * 1024], BF16)
        h_w = sbC("h_w", [128, 2 * NHC * 512], BF16)
        bcast_sb = sbC("bcast_sb", [128, QT], BF16)
        w_q = sbC("w_q", [16, 2 * QT], BF16)
        silu_g = sbC("silu_g", [128, 2 * 512], BF16)
        b_w = sbC("b_w", [128, 2 * 512], BF16)

        phaseA_done = [(ve, ve_A), (se, se_A), (pe, pe_A)] + b.lane_snap("scalar") + b.lane_snap("gpsimd")
        for eng in ENGS:
            b.waits(eng, phaseA_done)

        def w_srcs(e_):
            if e_ < NE:
                return (w1_d[e_].rearrange("(c p) h -> p c h", p=128),
                        w2_d[e_].rearrange("(c p) h -> p c h", p=128),
                        w3_d[e_].rearrange("(c p) h -> p c h", p=128))
            s = (e_ - NE) * HE
            return (ws1_d[:, s:s + HE].rearrange("(c p) h -> p c h", p=128),
                    ws2_d[:, s:s + HE].rearrange("(c p) h -> p c h", p=128),
                    ws3_d[s:s + HE, :].rearrange("(c p) h -> p c h", p=128))

        w1_ready = {}
        w2_ready = {}
        w3_ready = {}
        bank_free = {i: (pe, 0) for i in range(8)}
        ve_bcast = {}
        ystore = []
        hw_free = {0: (pe, 0), 1: (pe, 0)}
        yq_free = []
        lnq_par_free = {0: (pe, 0), 1: (pe, 0)}
        wq_free = {0: (pe, 0), 1: (pe, 0)}
        ln2_loaded = {}
        s1_free_val = {0: 0, 1: 0}

        seq = [(qq, ee) for qq in range(n_q) for ee in range(NPE)]
        cache_st = {}

        def prefetch(qn, en, npar, w1free, w2free, w3free):
            if qn == 0:
                g1, g2, g3 = w_srcs(en)
                w1_ready[(qn, en)] = b.dma(
                    "gpsimd", w1b[:, npar * NCC * HE:(npar + 1) * NCC * HE]
                    .rearrange("p (c h) -> p c h", c=NCC), g1, waits=w1free)
                w2_ready[(qn, en)] = b.dma(
                    "gpsimd", w2b[:].rearrange("p (c h) -> p c h", c=NCC), g2, waits=w2free)
                w3_ready[(qn, en)] = b.dma(
                    "gpsimd", w3b[:].rearrange("p (c h) -> p c h", c=NHC), g3, waits=w3free)
                sts = []
                sts.append(b.dma("sync", w1c[en], w1b[:, npar * NCC * HE:(npar + 1) * NCC * HE],
                                 waits=[w1_ready[(qn, en)]]))
                sts.append(b.dma("sync", w2c[en], w2b[:], waits=[w2_ready[(qn, en)]]))
                sts.append(b.dma("sync", w3c[en], w3b[:], waits=[w3_ready[(qn, en)]]))
                cache_st[en] = sts
            else:
                w1_ready[(qn, en)] = b.dma("sync", w1b[:, npar * NCC * HE:(npar + 1) * NCC * HE],
                                           w1c[en], waits=w1free + (cache_st[en][0:1] if qn == 1 else []))
                w2_ready[(qn, en)] = b.dma("sync", w2b[:], w2c[en],
                                           waits=w2free + (cache_st[en][1:2] if qn == 1 else []))
                w3_ready[(qn, en)] = b.dma("sync", w3b[:], w3c[en],
                                           waits=w3free + (cache_st[en][2:3] if qn == 1 else []))

        if n_q > 0:
            prefetch(0, 0, 0, [], [], [])

        wq_loaded = {}
        if n_q > 0:
            ln2_loaded[0] = b.dma("sync",
                                  ln2_q[:, 0:NCC * QT].rearrange("p (c t) -> p c t", c=NCC),
                                  ln2_dram[:, 0:QT].rearrange("(c p) t -> p c t", p=128),
                                  waits=list(stores[1]))
            wq_loaded[0] = b.dma("sync", w_q[:, 0:QT], w_dram[:, 0:QT], waits=list(w_stores[1]))
        for qq in range(n_q):
            tq = qq * QT
            lq = (qq % 2) * NCC * QT
            if qq + 1 < n_q:
                tn = (qq + 1) * QT
                wq_loaded[qq + 1] = b.dma("sync" if qq == 0 else "gpsimd",
                                          w_q[:, ((qq + 1) % 2) * QT:((qq + 1) % 2 + 1) * QT],
                                          w_dram[:, tn:tn + QT],
                                          waits=[wq_free[(qq + 1) % 2]] +
                                                list(w_stores[min(2 * qq + 3, len(w_stores) - 1)]))
                ln2_loaded[qq + 1] = b.dma(
                    "sync" if qq == 0 else "gpsimd",
                    ln2_q[:, ((qq + 1) % 2) * NCC * QT:((qq + 1) % 2 + 1) * NCC * QT]
                    .rearrange("p (c t) -> p c t", c=NCC),
                    ln2_dram[:, tn:tn + QT].rearrange("(c p) t -> p c t", p=128),
                    waits=[lnq_par_free[(qq + 1) % 2]] + list(stores[min(2 * qq + 3, len(stores) - 1)]))
            y_loaded = b.dma("sync" if qq == 0 else "gpsimd", y_q[:].rearrange("p (c t) -> p c t", c=NCC),
                             x1_dram[:, tq:tq + QT].rearrange("(c p) t -> p c t", p=128),
                             waits=yq_free + list(stores[2 * qq + 1]))

            for ee in range(NPE):
                idx = qq * NPE + ee
                w1par = idx % 2
                if ee < NE:
                    for tb in range(2):
                        b.op("tensor", lambda e, ee=ee, tb=tb, qq=qq:
                             e.matmul(B[6 + tb][:], selW[:, ee * 128:(ee + 1) * 128],
                                      w_q[:, (qq % 2) * QT + tb * TB: (qq % 2) * QT + (tb + 1) * TB],
                                      start=True, stop=True),
                             waits=[bank_free[6 + tb]] + ([wq_loaded[qq]] if (ee == 0 and tb == 0) else []),
                             inc=pe)
                        pe_b = pe.n
                        b.op("vector", lambda e, tb=tb:
                             e.tensor_copy(bcast_sb[:, tb * TB:(tb + 1) * TB], B[6 + tb][:]),
                             waits=[(pe, pe_b)], inc=ve)
                        bank_free[6 + tb] = (ve, ve.n)
                    ve_bcast[(qq, ee)] = (ve, ve.n)

                pe_hc = {}
                po_hc = {}
                po_h_tb = {}
                for tb in range(2):
                    for hc in range(NHC):
                        ghc = tb * NHC + hc
                        gb = ghc % 3
                        gbank = B[2 * gb]
                        bbank = B[2 * gb + 1]
                        wts = [bank_free[2 * gb], bank_free[2 * gb + 1]]
                        if ghc == 0:
                            wts += [w1_ready[(qq, ee)], w2_ready[(qq, ee)], ln2_loaded[qq]]
                        for cc in range(NCC):
                            b.op("tensor", lambda e, cc=cc, hc=hc, tb=tb, w1par=w1par, gbank=gbank, lq=lq:
                                 e.matmul(gbank[:],
                                          w1b[:, w1par * NCC * HE + cc * HE + hc * 128:
                                              w1par * NCC * HE + cc * HE + hc * 128 + 128],
                                          ln2_q[:, lq + cc * QT + tb * TB: lq + cc * QT + tb * TB + TB],
                                          start=(cc == 0), stop=(cc == NCC - 1)),
                                 waits=wts if cc == 0 else ())
                            wts = ()
                        for cc in range(NCC):
                            b.op("tensor", lambda e, cc=cc, hc=hc, tb=tb, bbank=bbank, lq=lq:
                                 e.matmul(bbank[:],
                                          w2b[:, cc * HE + hc * 128: cc * HE + hc * 128 + 128],
                                          ln2_q[:, lq + cc * QT + tb * TB: lq + cc * QT + tb * TB + TB],
                                          start=(cc == 0), stop=(cc == NCC - 1)),
                                 inc=pe if cc == NCC - 1 else None)
                        pe_hc[ghc] = pe.n
                        sl = ghc % 2
                        wts_s = [(pe, pe_hc[ghc])]
                        if ghc >= 2:
                            wts_s.append((po, po_hc[ghc - 2]))
                        b.op("scalar", lambda e, sl=sl, gbank=gbank:
                             e.activation(silu_g[:, sl * 512:(sl + 1) * 512], gbank[:], AF.Silu),
                             waits=wts_s, inc=se)
                        se_silu = se.n
                        wts_v = [(pe, pe_hc[ghc])]
                        if ghc >= 2:
                            wts_v.append((po, po_hc[ghc - 2]))
                        if ee < NE:
                            b.op("vector", lambda e, tb=tb, sl=sl, bbank=bbank:
                                 e.tensor_tensor(b_w[:, sl * 512:(sl + 1) * 512], bbank[:],
                                                 bcast_sb[:, tb * TB:(tb + 1) * TB], ALU.mult),
                                 waits=wts_v + [ve_bcast[(qq, ee)]], inc=ve)
                        else:
                            b.op("vector", lambda e, sl=sl, bbank=bbank:
                                 e.tensor_copy(b_w[:, sl * 512:(sl + 1) * 512], bbank[:]),
                                 waits=wts_v, inc=ve)
                        ve_bw = ve.n
                        wts_p = [(se, se_silu), (ve, ve_bw)]
                        if hc == 0:
                            wts_p.append(hw_free[tb])
                        b.op("gpsimd", lambda e, hc=hc, sl=sl, tb=tb:
                             e.tensor_tensor(h_w[:, tb * NHC * 512 + hc * 512: tb * NHC * 512 + (hc + 1) * 512],
                                             silu_g[:, sl * 512:(sl + 1) * 512],
                                             b_w[:, sl * 512:(sl + 1) * 512], ALU.mult),
                             waits=wts_p, inc=po)
                        po_hc[ghc] = po.n
                        bank_free[2 * gb] = (se, se_silu)
                        bank_free[2 * gb + 1] = (ve, ve_bw)
                    po_h_tb[tb] = po.n
                s1_free_val[w1par] = pe_hc[2 * NHC - 1]

                for tb in range(2):
                    for fc in range(NCC):
                        yb = B[6 + fc % 2]
                        wts = [bank_free[6 + fc % 2]]
                        if fc == 0:
                            wts += [(po, po_h_tb[tb]), w3_ready[(qq, ee)], y_loaded]
                        for hc in range(NHC):
                            b.op("tensor", lambda e, hc=hc, fc=fc, yb=yb, tb=tb:
                                 e.matmul(yb[:], w3b[:, hc * 1024 + fc * 128: hc * 1024 + fc * 128 + 128],
                                          h_w[:, tb * NHC * 512 + hc * 512: tb * NHC * 512 + (hc + 1) * 512],
                                          start=(hc == 0), stop=(hc == NHC - 1)),
                                 waits=wts if hc == 0 else (),
                                 inc=pe if hc == NHC - 1 else None)
                            wts = ()
                        pe_y = pe.n
                        b.op("vector", lambda e, fc=fc, tb=tb, yb=yb:
                             e.tensor_tensor(y_q[:, fc * QT + tb * TB: fc * QT + tb * TB + TB], yb[:],
                                             y_q[:, fc * QT + tb * TB: fc * QT + tb * TB + TB], ALU.add),
                             waits=[(pe, pe_y)], inc=ve)
                        bank_free[6 + fc % 2] = (ve, ve.n)
                    hw_free[tb] = (pe, pe.n)
                s2_done = ve.n

                if idx + 1 < len(seq):
                    qn, en = seq[idx + 1]
                    npar = (idx + 1) % 2
                    w1free = [(pe, s1_free_val[npar])]
                    w2free = [(pe, s1_free_val[w1par])]
                    w3free = [(ve, s2_done)]
                    if idx >= 1 and seq[idx - 1][1] in cache_st:
                        w1free += cache_st[seq[idx - 1][1]][0:1]
                    if ee in cache_st:
                        w2free += cache_st[ee][1:2]
                        w3free += cache_st[ee][2:3]
                    prefetch(qn, en, npar, w1free, w2free, w3free)

            y_ts = b.dma_multi("scalar",
                               [(out_fm[cc * 128:(cc + 1) * 128, tq:tq + QT],
                                 y_q[:, cc * QT:(cc + 1) * QT]) for cc in range(NCC)],
                               waits=[(ve, s2_done)])
            snap = b.lane_snap("scalar")
            ystore.append(snap)
            yq_free = snap
            lnq_par_free[qq % 2] = (pe, pe.n)
            wq_free[qq % 2] = (pe, pe.n)

        b.waits("gpsimd", b.lane_snap("scalar") + b.lane_snap("sync") + b.lane_snap("gpsimd"))

        with nc.Block() as block:
            @block.sync
            def _(eng):
                for f in b.q["sync"]:
                    f(eng)

            @block.tensor
            def _(eng):
                for f in b.q["tensor"]:
                    f(eng)

            @block.vector
            def _(eng):
                for f in b.q["vector"]:
                    f(eng)

            @block.scalar
            def _(eng):
                for f in b.q["scalar"]:
                    f(eng)

            @block.gpsimd
            def _(eng):
                for f in b.q["gpsimd"]:
                    f(eng)

    return nc


def host_constants():
    half = HD // 2
    theta = 1.0 / (10000.0 ** (np.arange(half, dtype=np.float64) / half))
    freqs = np.arange(T, dtype=np.float64)[:, None] * theta[None, :]
    cosf = np.cos(freqs)
    sinf = np.sin(freqs)
    p = np.arange(128)
    ii = p % 64
    idx = np.where(ii < 32, ii, ii - 32)
    sign = np.where(ii < 32, -1.0, 1.0)
    C128 = np.ascontiguousarray(cosf[:, idx].T).astype(BF)
    S128 = np.ascontiguousarray((sign[None, :] * sinf[:, idx]).T).astype(BF)
    permP = np.zeros((128, 128), dtype=BF)
    partner = np.where(ii < 32, p + 32, p - 32)
    permP[p, partner] = 1.0
    identB = np.eye(128, dtype=BF)
    selAB = np.zeros((2, 256), dtype=BF)
    selAB[0, 0:128] = 1.0
    selAB[1, 128:256] = 1.0
    selW = np.zeros((16, NE * 128), dtype=BF)
    for e_ in range(NE):
        selW[e_, e_ * 128:(e_ + 1) * 128] = 1.0
        selW[NE + e_, e_ * 128:(e_ + 1) * 128] = 1.0
    jj = np.arange(128)[:, None]
    ti = np.arange(128)[None, :]
    mhi = ((jj + 1 <= ti) & (ti <= jj + 64)).astype(BF)
    jl = np.arange(64)[:, None]
    mlo = (np.arange(128)[None, :] <= jl).astype(BF)
    maskhi2 = np.ascontiguousarray(np.concatenate([mhi] * 4, axis=1))
    masklo2 = np.ascontiguousarray(np.concatenate([mlo] * 4, axis=1))
    return dict(cosT=C128, sinT=S128, permP=permP, identB=identB, selAB=selAB, selW=selW,
                maskhi2=maskhi2, masklo2=masklo2)


_cache = {}


def kernel(L0_attn_x, L0_attn_wq, L0_attn_wk, L0_attn_wv, L0_attn_wo, L0_attn_norm_scale,
           L0_ffn_Wg, L0_ffn_W1, L0_ffn_W2, L0_ffn_W3, L0_ffn_Ws1, L0_ffn_Ws2, L0_ffn_Ws3):
    x = np.ascontiguousarray(np.asarray(L0_attn_x, dtype=np.float32))
    gamma = np.asarray(L0_attn_norm_scale, dtype=np.float32)
    gamma_fm = np.ascontiguousarray(gamma.reshape(NCC, 128).T)
    shared = dict(
        wq=np.ascontiguousarray(np.asarray(L0_attn_wq, np.float32)),
        wk=np.ascontiguousarray(np.asarray(L0_attn_wk, np.float32)),
        wv=np.ascontiguousarray(np.asarray(L0_attn_wv, np.float32)),
        wo=np.ascontiguousarray(np.asarray(L0_attn_wo, np.float32)),
        gamma_fm=gamma_fm,
        wg=np.ascontiguousarray(np.asarray(L0_ffn_Wg, np.float32)),
        w1=np.ascontiguousarray(np.asarray(L0_ffn_W1, np.float32)),
        w2=np.ascontiguousarray(np.asarray(L0_ffn_W2, np.float32)),
        w3=np.ascontiguousarray(np.asarray(L0_ffn_W3, np.float32)),
        ws1=np.ascontiguousarray(np.asarray(L0_ffn_Ws1, np.float32)),
        ws2=np.ascontiguousarray(np.asarray(L0_ffn_Ws2, np.float32)),
        ws3=np.ascontiguousarray(np.asarray(L0_ffn_Ws3, np.float32)),
        **host_constants(),
    )
    if "nc" not in _cache:
        _cache["nc"] = build_program()
    nc = _cache["nc"]
    in_maps = []
    for b_ in range(8):
        m = dict(shared)
        m["x_fm"] = np.ascontiguousarray(x[b_].T)
        in_maps.append(m)
    res = run_bass_kernel_spmd(nc, in_maps, core_ids=list(range(8)))
    out = np.empty((8, T, D), dtype=np.float32)
    for b_ in range(8):
        out[b_] = np.asarray(res.results[b_]["out_fm"], dtype=np.float32).T
    return out


# version 17
# speedup vs baseline: 1.0023x; 1.0023x over previous
import numpy as np
import ml_dtypes
from contextlib import ExitStack

import concourse.bass as bass
import concourse.mybir as mybir
from concourse.bass_utils import run_bass_kernel_spmd

F32 = mybir.dt.float32
BF16 = mybir.dt.bfloat16
AF = mybir.ActivationFunctionType
ALU = mybir.AluOpType
BF = ml_dtypes.bfloat16

D = 1024
T = 4096
NH = 16
HD = 64
NE = 8
HE = 1408
NCC = 8
TILE = 512
NTILE = 8
QT = 1024
NQ = 4
TB = 512
NHC = 11
NPE = 10

ENGS = ("sync", "tensor", "vector", "scalar", "gpsimd")


class Sem:
    __slots__ = ("h", "n", "name", "waited_max", "issuer_synced")

    def __init__(self, h, name):
        self.h = h
        self.n = 0
        self.name = name
        self.waited_max = 0
        self.issuer_synced = 0


class Build:
    def __init__(self, nc, ctx):
        self.nc = nc
        self.ctx = ctx
        self.q = {k: [] for k in ENGS}

    def sem(self, name):
        return Sem(self.ctx.enter_context(self.nc.semaphore(name)), name)

    def op(self, eng, fn, waits=(), inc=None, amt=1):
        wl = tuple((s, int(v)) for s, v in waits if v > 0)
        for s, v in wl:
            assert v <= s.n, f"wait on future {s.name}:{v} > {s.n}"
            s.waited_max = max(s.waited_max, v)
        if inc is not None:
            inc.n += amt

        def run(e, fn=fn, wl=wl, inc=inc, amt=amt):
            for s, v in wl:
                e.wait_ge(s.h, v)
            inst = fn(e)
            if inc is not None:
                inst.then_inc(inc.h, amt)

        self.q[eng].append(run)

    def waits(self, eng, waits):
        wl = tuple((s, int(v)) for s, v in waits if v > 0)
        for s, v in wl:
            assert v <= s.n
            s.waited_max = max(s.waited_max, v)
        if wl:
            self.q[eng].append(lambda e, wl=wl: [e.wait_ge(s.h, v) for s, v in wl])

    def bump(self, eng, sem, amt=1):
        sem.n += amt
        self.q[eng].append(lambda e, sem=sem, amt=amt: e.sem_inc(sem.h, amt))

    def drain(self, eng):
        self.q[eng].append(lambda e: e.drain())

    def make_lanes(self, spec):
        self.lanes = {eng: [self.sem(f"q_{eng}{i}") for i in range(n)] for eng, n in spec.items()}
        self.lane_i = {eng: 0 for eng in spec}

    def dma(self, eng, dst, src, waits=()):
        lane = self.lanes[eng][self.lane_i[eng] % len(self.lanes[eng])]
        self.lane_i[eng] += 1
        extra = [(lane, lane.n)] if lane.n > lane.issuer_synced else []
        lane.issuer_synced = max(lane.issuer_synced, lane.n)
        self.op(eng, lambda e, d=dst, s=src: e.dma_start(d, s),
                waits=list(waits) + extra, inc=lane, amt=16)
        return (lane, lane.n)

    def lane_snap(self, eng):
        return [(lane, lane.n) for lane in self.lanes[eng] if lane.n > 0]

    def dma_multi(self, eng, pairs, waits=()):
        d = {}
        for i, (dst, srcv) in enumerate(pairs):
            s, v = self.dma(eng, dst, srcv, waits=waits if i == 0 else ())
            d[s] = max(d.get(s, 0), v)
        return list(d.items())


def build_program(debug=False, n_tiles=NTILE, n_q=NQ, stages=5):
    nc = bass.Bass("TRN2", target_bir_lowering=False)

    dr = lambda n, sh, dt, kind: nc.dram_tensor(n, sh, dt, kind=kind).ap()
    x_fm = dr("x_fm", [D, T], F32, "ExternalInput")
    wq_d = dr("wq", [D, D], F32, "ExternalInput")
    wk_d = dr("wk", [D, D], F32, "ExternalInput")
    wv_d = dr("wv", [D, D], F32, "ExternalInput")
    wo_d = dr("wo", [D, D], F32, "ExternalInput")
    gamma_d = dr("gamma_fm", [128, NCC], F32, "ExternalInput")
    wg_d = dr("wg", [D, NE], F32, "ExternalInput")
    w1_d = dr("w1", [NE, D, HE], F32, "ExternalInput")
    w2_d = dr("w2", [NE, D, HE], F32, "ExternalInput")
    w3_d = dr("w3", [NE, HE, D], F32, "ExternalInput")
    ws1_d = dr("ws1", [D, 2 * HE], F32, "ExternalInput")
    ws2_d = dr("ws2", [D, 2 * HE], F32, "ExternalInput")
    ws3_d = dr("ws3", [2 * HE, D], F32, "ExternalInput")
    cos_d = dr("cosT", [128, T], BF16, "ExternalInput")
    sin_d = dr("sinT", [128, T], BF16, "ExternalInput")
    perm_d = dr("permP", [128, 128], BF16, "ExternalInput")
    ident_d = dr("identB", [128, 128], BF16, "ExternalInput")
    selab_d = dr("selAB", [2, 256], BF16, "ExternalInput")
    selw_d = dr("selW", [16, NE * 128], BF16, "ExternalInput")
    maskhi_d = dr("maskhi2", [128, 512], BF16, "ExternalInput")
    masklo_d = dr("masklo2", [64, 512], BF16, "ExternalInput")

    out_fm = dr("out_fm", [D, T], F32, "ExternalOutput")
    x1_dram = dr("x1s", [D, T], F32, "Internal")
    w_dram = dr("ws_gates", [16, T], BF16, "Internal")
    w1c = dr("w1c", [NPE, 128, NCC * HE], BF16, "Internal")
    w2c = dr("w2c", [NPE, 128, NCC * HE], BF16, "Internal")
    w3c = dr("w3c", [NPE, 128, NHC * 1024], BF16, "Internal")
    ln2_dram = dr("ln2s", [D, T], BF16, "Internal")
    if debug:
        x1_dbg = dr("x1_dbg", [D, T], F32, "ExternalOutput")
        ln2_dbg = dr("ln2_dbg", [D, T], BF16, "ExternalOutput")
        w_dbg = dr("w_dbg", [16, T], BF16, "ExternalOutput")

    ctx = ExitStack()
    with ctx:
        b = Build(nc, ctx)
        sb = lambda n, sh, dt: ctx.enter_context(nc.sbuf_tensor(n, sh, dt))
        psum = lambda n, sh: ctx.enter_context(nc.psum_tensor(n, sh, F32))

        b.make_lanes({"sync": 3, "gpsimd": 3, "scalar": 4})
        pe = b.sem("pe")
        ve = b.sem("ve")
        se = b.sem("se")
        po = b.sem("po")

        wg32 = sb("wg32", [128, NCC * NE], F32)
        gamma = sb("gamma", [128, NCC], F32)
        permP = sb("permP_sb", [128, 128], BF16)
        identB = sb("identB_sb", [128, 128], BF16)
        selAB = sb("selAB_sb", [2, 256], BF16)
        selW = sb("selW_sb", [16, NE * 128], BF16)
        maskhi = sb("maskhi_sb", [128, 512], BF16)
        masklo = sb("masklo_sb", [64, 512], BF16)
        onesc = sb("onesc", [128, 1], BF16)
        ones1f = sb("ones1f", [1, 128], F32)
        cs_row = sb("cs_row", [1, 8], F32)
        cs_rep = sb("cs_rep", [128, 8], F32)
        ab_tm = sb("ab_tm", [128, 8], F32)
        onesc64 = sb("onesc64", [128, 1], BF16)
        ones1 = sb("ones1", [1, 128], BF16)
        onef = sb("onef", [128, 1], F32)
        epsln = sb("epsln", [1, 1], F32)
        epsrms = sb("epsrms", [1, 1], F32)
        w_fmt = sb("w_fmt", [16, 512], BF16)

        B = [psum(f"B{i}", [128, 512]) for i in range(8)]

        stA = ExitStack()
        sbA = lambda n, sh, dt: stA.enter_context(nc.sbuf_tensor(n, sh, dt))
        wq_bf = sbA("wq_bf", [128, NCC * 1024], BF16)
        wk_bf = sbA("wk_bf", [128, NCC * 1024], BF16)
        wv_bf = sbA("wv_bf", [128, NCC * 1024], BF16)
        wo_bf = sbA("wo_bf", [128, NCC * 1024], BF16)
        x_sb2 = sbA("x_sb2", [128, 2 * NCC * 512], F32)
        x_sb = x_sb2
        ln1_32 = sbA("ln1_32", [128, NCC * 512], F32)
        arena_bf = sbA("arena_bf", [128, 8192], BF16)
        arena32 = arena_bf[:].bitcast(F32)
        ln1_bf = sbA("ln1_bf", [128, NCC * 512], BF16)
        qA = sbA("qA", [64, NCC * 512], BF16)
        qB = sbA("qB", [64, NCC * 512], BF16)
        kA = sbA("kA", [64, NCC * 512], BF16)
        kB = sbA("kB", [64, NCC * 512], BF16)
        kA_halo = sbA("kA_halo", [64, NCC * 64], BF16)
        kB_halo = sbA("kB_halo", [64, NCC * 64], BF16)
        msg_fm = ln1_bf
        v_tm = sbA("v_tm", [128, 4 * 1024], BF16)
        v_lo = sbA("v_lo", [128, 1024], BF16)
        tau_hi = sbA("tau_hi", [128, 16 * 128], BF16)
        tau_lo = sbA("tau_lo", [128, 16 * 128], BF16)
        msg_n = sbA("msg_n", [128, 1024], BF16)
        q_raw = tau_hi
        t1 = msg_n
        t2 = tau_lo
        ropeC = sbA("ropeC", [128, 512], BF16)
        ropeS = sbA("ropeS", [128, 512], BF16)
        rows32 = arena32[0:1, 2048:3584]
        rows_bf_t = sbA("rows_bf", [1, 1024], BF16)
        rows_bf = rows_bf_t[0:1, :]
        mass_sb = sbA("mass_sb", [128, 16], F32)
        recip_sb = sbA("recip_sb", [128, 16], F32)
        r_sb = sbA("r_sb", [128, 32], F32)
        srt_sb = sbA("srt_sb", [128, 32], F32)
        gd_sb = sbA("gd_sb", [128, 4], F32)
        p2_sb = sbA("p2_sb", [128, 4], F32)
        pd_sb = sbA("pd_sb", [128, 4], F32)
        wtmp1 = sbA("wtmp1", [128, 32], F32)
        wtmp2 = sbA("wtmp2", [128, 32], F32)
        w32 = sbA("w32", [128, 32], F32)
        wstack = sbA("wstack", [128, 64], BF16)

        b.dma("sync", gamma[:], gamma_d)
        b.dma("sync", wg32[:].rearrange("p (c e) -> p c e", c=NCC),
              wg_d.rearrange("(c p) e -> p c e", p=128))
        b.dma("sync", permP[:], perm_d)
        b.dma("sync", identB[:], ident_d)
        b.dma("sync", selAB[:], selab_d)
        b.dma("sync", selW[:], selw_d)
        b.dma("sync", maskhi[:], maskhi_d)
        b.dma("sync", masklo[:], masklo_d)
        b.op("vector", lambda e: e.memset(onesc[:], 1.0))
        b.op("vector", lambda e: e.memset(ones1f[:], 1.0))
        b.op("vector", lambda e: e.memset(onesc64[:], 0.0))
        b.drain("vector")
        b.op("vector", lambda e: e.memset(onesc64[0:64, :], 1.0))
        b.op("vector", lambda e: e.memset(ones1[:], 1.0))
        b.op("vector", lambda e: e.memset(onef[:], 1.0))
        b.op("vector", lambda e: e.memset(epsln[:], 1e-5))
        b.op("vector", lambda e: e.memset(epsrms[:], 1e-6))
        b.op("vector", lambda e: e.memset(tau_lo[:], 0.0))
        b.op("vector", lambda e: e.memset(v_lo[:], 0.0), inc=ve)
        ve_memset = ve.n

        cast_prev = 0
        for wi, (wdram, wsb) in enumerate([(wq_d, wq_bf), (wk_d, wk_bf), (wv_d, wv_bf), (wo_d, wo_bf)]):
            for half in range(2):
                for j in range(4):
                    cc = half * 4 + j
                    t_ = b.dma("sync", x_sb[:, j * 1024:(j + 1) * 1024],
                               wdram[cc * 128:(cc + 1) * 128, :],
                               waits=[(se, cast_prev)] if j == 0 else ())
                    b.op("scalar", lambda e, j=j, cc=cc, wsb=wsb:
                         e.copy(wsb[:, cc * 1024:(cc + 1) * 1024], x_sb[:, j * 1024:(j + 1) * 1024]),
                         waits=[t_], inc=se)
                cast_prev = se.n
        se_setup = se.n
        qw_setup0 = b.lane_snap("sync")
        b.waits("vector", qw_setup0)
        for cc in range(NCC):
            b.op("tensor", lambda e, cc=cc:
                 e.matmul(B[6][0:1, 0:8], onef[:], wg32[:, cc * 8:(cc + 1) * 8],
                          start=(cc == 0), stop=(cc == NCC - 1)),
                 waits=(qw_setup0 + [(ve, ve.n)]) if cc == 0 else (),
                 inc=pe if cc == NCC - 1 else None)
        b.op("vector", lambda e: e.tensor_copy(cs_row[:], B[6][0:1, 0:8]), waits=[(pe, pe.n)], inc=ve)
        b.op("tensor", lambda e: e.matmul(B[6][:, 0:8], ones1f[:], cs_row[:], start=True, stop=True),
             waits=[(ve, ve.n)], inc=pe)
        b.op("vector", lambda e: e.tensor_copy(cs_rep[:], B[6][:, 0:8]), waits=[(pe, pe.n)], inc=ve)
        ve_csrep = ve.n
        qw_setup = b.lane_snap("sync")
        b.waits("tensor", qw_setup + [(ve, ve_memset)])
        b.waits("vector", qw_setup)

        xsb_free = {0: (ve, 0), 1: (ve, 0)}
        ln132_free = []
        ln1bf_free = []
        qbf_free = (pe, 0)
        kbf_free = (pe, 0)
        rope_free = [(ve, 0)]
        kv_halo_ready = (ve, ve_memset)
        bank_free = {i: (pe, 0) for i in range(8)}
        vtm_free = (pe, 0)
        stores = []
        w_stores = []
        w_fmt_free = []
        x1_store_q = []
        ve_ln2_prev = 0
        pe_st2_prev = 0
        ve_x1_prev = 0

        for i in range(n_tiles):
            t0 = i * TILE
            xo = (i % 2) * NCC * 512
            qx_x = b.dma("gpsimd", x_sb2[:, xo:xo + NCC * 512].rearrange("p (c t) -> p c t", c=NCC),
                         x_fm[:, t0:t0 + TILE].rearrange("(c p) t -> p c t", p=128),
                         waits=[xsb_free[i % 2], (se, se_setup if i == 0 else 0)])
            rp = 0
            ct_t = b.dma("sync", ropeC[:, 0:512], cos_d[:, t0:t0 + TILE],
                         waits=[rope_free[0]])
            st_t = b.dma("sync", ropeS[:, 0:512], sin_d[:, t0:t0 + TILE])

            se_stats0 = se.n
            for cc in range(NCC):
                b.op("scalar", lambda e, cc=cc, xo=xo:
                     e.copy(arena_bf[:, cc * 512:(cc + 1) * 512], x_sb2[:, xo + cc * 512:xo + (cc + 1) * 512]),
                     waits=[qx_x, (pe, pe_st2_prev), (ve, ve_x1_prev)] if cc == 0 else ())
                b.op("scalar", lambda e, cc=cc, xo=xo:
                     e.activation(arena_bf[:, 4096 + cc * 512:4096 + (cc + 1) * 512],
                                  x_sb2[:, xo + cc * 512:xo + (cc + 1) * 512], AF.Square),
                     inc=se)
            se_stats = se.n

            def ln_stats():
                for cc in range(NCC):
                    b.op("tensor", lambda e, cc=cc:
                         e.matmul(B[4][0:1, :], onesc[:], arena_bf[:, cc * 512:(cc + 1) * 512],
                                  start=(cc == 0), stop=(cc == NCC - 1)),
                         waits=([bank_free[4], bank_free[5], (se, min(se_stats0 + cc + 1, se_stats))]
                                if cc == 0 else [(se, min(se_stats0 + cc + 1, se_stats))]))
                    b.op("tensor", lambda e, cc=cc:
                         e.matmul(B[5][0:1, :], onesc[:], arena_bf[:, 4096 + cc * 512:4096 + (cc + 1) * 512],
                                  start=(cc == 0), stop=(cc == NCC - 1)),
                         inc=pe if cc == NCC - 1 else None)
                return pe.n

            def ln_rowops(pe_stats_v):
                R0 = rows32[0:1, 0:512]
                R1 = rows32[0:1, 512:1024]
                R2 = rows32[0:1, 1024:1536]
                b.op("vector", lambda e: e.tensor_scalar(R0, B[4][0:1, :], 1.0 / D, None, ALU.mult),
                     waits=[(pe, pe_stats_v)])
                b.drain("vector")
                b.op("vector", lambda e: e.tensor_tensor(R1, R0, R0, ALU.mult))
                b.drain("vector")
                b.op("vector", lambda e: e.scalar_tensor_tensor(R1, B[5][0:1, :], 1.0 / D,
                                                                R1, ALU.mult, ALU.subtract),
                     inc=ve)
                b.op("scalar", lambda e: e.activation(R1, R1, AF.Ln, bias=epsln[0:1, :]),
                     waits=[(ve, ve.n)])
                b.drain("scalar")
                b.op("scalar", lambda e: e.activation(R2, R1, AF.Exp, scale=-0.5), inc=se)
                b.op("vector", lambda e: e.tensor_copy(rows_bf[0:1, 0:512], R2), waits=[(se, se.n)])
                b.op("vector", lambda e: e.scalar_tensor_tensor(R1, R0, -1.0, R2, ALU.mult, ALU.mult))
                b.drain("vector")
                b.op("vector", lambda e: e.tensor_copy(rows_bf[0:1, 512:1024], R1), inc=ve)
                return ve.n

            pe_st = ln_stats()
            ve_rows = ln_rowops(pe_st)
            b.op("tensor", lambda e: e.matmul(B[4][:], ones1[:], rows_bf[0:1, 0:512], start=True, stop=True),
                 waits=[(ve, ve_rows)])
            b.op("tensor", lambda e: e.matmul(B[5][:], ones1[:], rows_bf[0:1, 512:1024], start=True, stop=True),
                 inc=pe)
            pe_bc = pe.n
            for cc in range(NCC):
                b.op("vector", lambda e, cc=cc, xo=xo:
                     e.tensor_tensor(ln1_32[:, cc * 512:(cc + 1) * 512],
                                     x_sb2[:, xo + cc * 512:xo + (cc + 1) * 512],
                                     B[4][:], ALU.mult),
                     waits=([(pe, pe_bc)] + ln132_free) if cc == 0 else ())
            b.drain("vector")
            ve_ln1_base = ve.n
            for cc in range(NCC):
                b.op("vector", lambda e, cc=cc:
                     e.tensor_tensor(ln1_32[:, cc * 512:(cc + 1) * 512], ln1_32[:, cc * 512:(cc + 1) * 512],
                                     B[5][:], ALU.add),
                     inc=ve)
            ve_ln1 = ve.n
            bank_free[4] = (ve, ve_ln1)
            bank_free[5] = (ve, ve_ln1)
            se_ln1_base = se.n
            for cc in range(NCC):
                b.op("scalar", lambda e, cc=cc:
                     e.copy(ln1_bf[:, cc * 512:(cc + 1) * 512], ln1_32[:, cc * 512:(cc + 1) * 512]),
                     waits=([(ve, ve_ln1_base + cc + 1)] + (ln1bf_free if cc == 0 else [])),
                     inc=se)
            se_ln1 = se.n

            if stages < 2:
                continue
            keys = [(ti, fc) for ti in range(2) for fc in range(NCC)]
            se_evac = {}
            pe_perm = {}
            ve_rope = {}

            def emit_projection(kidx):
                ti, fc = keys[kidx]
                wsb = wq_bf if ti == 0 else wk_bf
                bp = B[kidx % 2]
                for cc in range(NCC):
                    wts = []
                    if kidx == 0:
                        wts = [(se, se_ln1_base + cc + 1)]
                        if cc == 0:
                            wts += [bank_free[kidx % 2], qbf_free, kbf_free]
                    elif cc == 0:
                        wts = [(se, se_ln1), bank_free[kidx % 2]]
                    b.op("tensor", lambda e, cc=cc, fc=fc, wsb=wsb, bp=bp:
                         e.matmul(bp[:], wsb[:, cc * 1024 + fc * 128: cc * 1024 + fc * 128 + 128],
                                  ln1_bf[:, cc * 512:(cc + 1) * 512],
                                  start=(cc == 0), stop=(cc == NCC - 1)),
                         waits=wts,
                         inc=pe if cc == NCC - 1 else None)
                pe_proj = pe.n
                wts2 = [(pe, pe_proj)]
                if kidx >= 2:
                    wts2.append(ve_rope[kidx - 2])
                b.op("scalar", lambda e, kidx=kidx, bp=bp:
                     e.copy(q_raw[:, 1024 + (kidx % 2) * 512:1024 + ((kidx % 2) + 1) * 512], bp[:]),
                     waits=wts2, inc=se)
                se_evac[kidx] = (se, se.n)
                bank_free[kidx % 2] = (se, se.n)
                b.op("tensor", lambda e, kidx=kidx:
                     e.matmul(B[2 + kidx % 2][:], permP[:],
                              q_raw[:, 1024 + (kidx % 2) * 512:1024 + ((kidx % 2) + 1) * 512], start=True, stop=True),
                     waits=[se_evac[kidx], bank_free[2 + kidx % 2]], inc=pe)
                pe_perm[kidx] = (pe, pe.n)

            for kp in range(0, 16, 2):
                emit_projection(kp)
                emit_projection(kp + 1)
                for kidx in (kp, kp + 1):
                    ti, fc = keys[kidx]
                    b.op("vector", lambda e, kidx=kidx, rp=rp:
                         e.tensor_tensor(t1[:, (kidx % 2) * 512:((kidx % 2) + 1) * 512],
                                         q_raw[:, 1024 + (kidx % 2) * 512:1024 + ((kidx % 2) + 1) * 512],
                                         ropeC[:, rp * 512:(rp + 1) * 512], ALU.mult),
                         waits=[se_evac[kidx], ct_t])
                    b.op("vector", lambda e, kidx=kidx, rp=rp:
                         e.tensor_tensor(t2[:, (kidx % 2) * 512:((kidx % 2) + 1) * 512],
                                         B[2 + kidx % 2][:],
                                         ropeS[:, rp * 512:(rp + 1) * 512], ALU.mult),
                         waits=[pe_perm[kidx], st_t])
                b.drain("vector")
                for kidx in (kp, kp + 1):
                    ti, fc = keys[kidx]
                    dstA = qA if ti == 0 else kA
                    dstB = qB if ti == 0 else kB
                    wts = []
                    if kidx == 0:
                        wts = [qbf_free]
                    elif kidx == NCC:
                        wts = [kbf_free]
                    b.op("vector", lambda e, fc=fc, kidx=kidx, dstA=dstA:
                         e.tensor_tensor(dstA[0:64, fc * 512:(fc + 1) * 512],
                                         t1[0:64, (kidx % 2) * 512:((kidx % 2) + 1) * 512],
                                         t2[0:64, (kidx % 2) * 512:((kidx % 2) + 1) * 512], ALU.add),
                         waits=wts)
                    b.op("vector", lambda e, fc=fc, kidx=kidx, dstB=dstB:
                         e.tensor_tensor(dstB[0:64, fc * 512:(fc + 1) * 512],
                                         t1[64:128, (kidx % 2) * 512:((kidx % 2) + 1) * 512],
                                         t2[64:128, (kidx % 2) * 512:((kidx % 2) + 1) * 512], ALU.add),
                         inc=ve)
                    ve_rope[kidx] = (ve, ve.n)
                    bank_free[2 + kidx % 2] = (ve, ve.n)
            ve_rope_all = ve.n
            rope_free[0] = (ve, ve_rope_all)

            ve_v = {}
            for tb2 in range(4):
                for half in range(2):
                    bk = (tb2 * 2 + half) % 4
                    for cc in range(NCC):
                        b.op("tensor", lambda e, cc=cc, tb2=tb2, half=half, bk=bk:
                             e.matmul(B[bk][:],
                                      ln1_bf[:, cc * 512 + tb2 * 128: cc * 512 + tb2 * 128 + 128],
                                      wv_bf[:, cc * 1024 + half * 512: cc * 1024 + half * 512 + 512],
                                      start=(cc == 0), stop=(cc == NCC - 1)),
                             waits=[bank_free[bk], vtm_free] if cc == 0 else (),
                             inc=pe if cc == NCC - 1 else None)
                    pe_vp = pe.n
                    b.op("vector", lambda e, tb2=tb2, half=half, bk=bk:
                         e.tensor_copy(v_tm[:, tb2 * 1024 + half * 512: tb2 * 1024 + half * 512 + 512],
                                       B[bk][:]),
                         waits=[(pe, pe_vp)], inc=ve)
                    bank_free[bk] = (ve, ve.n)
                ve_v[tb2] = (ve, ve.n)

            if stages < 3:
                continue
            pe_msg_prev = 0
            pe_msg_all = 0
            se_msgn_prev = None
            for qb in range(4):
                lo_valid = not (i == 0 and qb == 0)
                if qb >= 1:
                    b.op("vector", lambda e, qb=qb:
                         e.tensor_copy(v_lo[0:64, :], v_tm[64:128, (qb - 1) * 1024: qb * 1024]),
                         waits=[ve_v[qb - 1], (pe, pe_msg_prev)], inc=ve)
                ve_vlo = ve.n

                mass_pending = []
                for burst in range(2):
                    for hj in range(8):
                        h = 8 * burst + hj
                        fc = h // 2
                        hibank = B[2 + hj // 4]
                        lobank = B[4 + hj // 4]
                        col = (hj % 4) * 128
                        ksrc = kA if h % 2 == 0 else kB
                        qsrc = qA if h % 2 == 0 else qB
                        khalo = kA_halo if h % 2 == 0 else kB_halo
                        wts = []
                        if hj == 0 or hj == 4:
                            wts = [bank_free[2 + hj // 4], bank_free[4 + hj // 4]]
                            if qb == 0 and burst == 0 and hj == 0:
                                wts += [(ve, ve_rope_all), kv_halo_ready]
                        b.op("tensor", lambda e, fc=fc, qb=qb, hibank=hibank, col=col, ksrc=ksrc, qsrc=qsrc:
                             e.matmul(hibank[:, col:col + 128],
                                      ksrc[0:64, fc * 512 + qb * 128: fc * 512 + qb * 128 + 128],
                                      qsrc[0:64, fc * 512 + qb * 128: fc * 512 + qb * 128 + 128],
                                      start=True, stop=True),
                             waits=wts,
                             inc=pe if (hj == 7 and not lo_valid) else None)
                        if lo_valid:
                            if qb == 0:
                                ks = khalo[0:64, fc * 64:(fc + 1) * 64]
                            else:
                                ks = ksrc[0:64, fc * 512 + qb * 128 - 64: fc * 512 + qb * 128]
                            b.op("tensor", lambda e, lobank=lobank, col=col, ks=ks, fc=fc, qb=qb, qsrc=qsrc:
                                 e.matmul(lobank[0:64, col:col + 128], ks,
                                          qsrc[0:64, fc * 512 + qb * 128: fc * 512 + qb * 128 + 128],
                                          start=True, stop=True),
                                 inc=pe if hj == 7 else None)
                    pe_sc = pe.n

                    if burst == 0 and se_msgn_prev is not None:
                        sm, pqb = se_msgn_prev
                        for fc in range(NCC):
                            b.op("tensor", lambda e, fc=fc:
                                 e.transpose(B[1][:, fc * 64:fc * 64 + 64].bitcast(BF16),
                                             msg_n[:, fc * 128:(fc + 1) * 128], identB[:]),
                                 waits=[(se, sm), bank_free[1]] if fc == 0 else (),
                                 inc=pe if fc == NCC - 1 else None)
                        pe_tr = pe.n
                        for fc in range(NCC):
                            b.op("vector", lambda e, fc=fc, pqb=pqb:
                                 e.tensor_copy(msg_fm[:, fc * 512 + pqb * 128: fc * 512 + pqb * 128 + 128],
                                               B[1][:, fc * 64:fc * 64 + 64].bitcast(BF16)),
                                 waits=[(pe, pe_tr)] if fc == 0 else (),
                                 inc=ve if fc == NCC - 1 else None)
                        bank_free[1] = (ve, ve.n)
                        se_msgn_prev = None

                    sig_half = {}
                    for half in range(2):
                        hb = B[2 + half]
                        lb = B[4 + half]
                        b.op("scalar", lambda e, burst=burst, half=half, hb=hb:
                             e.activation(tau_hi[:, burst * 1024 + half * 512: burst * 1024 + (half + 1) * 512],
                                          hb[:], AF.Sigmoid, scale=0.125),
                             waits=[(pe, pe_sc)] if half == 0 else (),
                             inc=None if lo_valid else se)
                        if lo_valid:
                            b.op("scalar", lambda e, burst=burst, half=half, lb=lb:
                                 e.activation(tau_lo[0:64, burst * 1024 + half * 512: burst * 1024 + (half + 1) * 512],
                                              lb[0:64, :], AF.Sigmoid, scale=0.125),
                                 inc=se)
                        sig_half[half] = (se, se.n)
                        bank_free[2 + half] = (se, se.n)
                        bank_free[4 + half] = (se, se.n)
                    mask_half = {}
                    for half in range(2):
                        b.op("vector", lambda e, burst=burst, half=half:
                             e.tensor_tensor(tau_hi[:, burst * 1024 + half * 512: burst * 1024 + (half + 1) * 512],
                                             tau_hi[:, burst * 1024 + half * 512: burst * 1024 + (half + 1) * 512],
                                             maskhi[:], ALU.mult),
                             waits=[sig_half[half]],
                             inc=None if lo_valid else ve)
                        if lo_valid:
                            b.op("vector", lambda e, burst=burst, half=half:
                                 e.tensor_tensor(tau_lo[0:64, burst * 1024 + half * 512: burst * 1024 + (half + 1) * 512],
                                                 tau_lo[0:64, burst * 1024 + half * 512: burst * 1024 + (half + 1) * 512],
                                                 masklo[:], ALU.mult),
                                 inc=ve)
                        mask_half[half] = (ve, ve.n)

                    for mi, (hm, lo_m) in enumerate(mass_pending):
                        mwts = [bank_free[0]] if mi == 0 else []
                        if lo_m:
                            b.op("tensor", lambda e, h=hm:
                                 e.matmul(B[0][:, h:h + 1], tau_lo[:, h * 128:(h + 1) * 128], onesc64[:],
                                          start=True, stop=False),
                                 waits=mwts)
                            mwts = []
                        b.op("tensor", lambda e, h=hm, lo=lo_m:
                             e.matmul(B[0][:, h:h + 1], tau_hi[:, h * 128:(h + 1) * 128], onesc[:],
                                      start=(not lo), stop=True),
                             waits=mwts)
                    mass_pending = []

                    for hj in range(8):
                        h = 8 * burst + hj
                        mb = B[6 + h // 8]
                        wts = []
                        if hj == 0:
                            wts = [mask_half[0], bank_free[6 + burst]]
                            if burst == 0:
                                wts += [(ve, ve_vlo)]
                        elif hj == 4:
                            wts = [mask_half[1]]
                        if lo_valid:
                            b.op("tensor", lambda e, h=h, mb=mb:
                                 e.matmul(mb[:, (h % 8) * 64:((h % 8) + 1) * 64],
                                          tau_lo[:, h * 128:(h + 1) * 128], v_lo[:, h * 64:(h + 1) * 64],
                                          start=True, stop=False),
                                 waits=wts)
                            wts = []
                        b.op("tensor", lambda e, h=h, mb=mb, qb=qb, lo=lo_valid:
                             e.matmul(mb[:, (h % 8) * 64:((h % 8) + 1) * 64],
                                      tau_hi[:, h * 128:(h + 1) * 128],
                                      v_tm[:, qb * 1024 + h * 64: qb * 1024 + h * 64 + 64],
                                      start=(not lo), stop=True),
                             waits=wts)
                        mass_pending.append((h, lo_valid))
                for mi, (hm, lo_m) in enumerate(mass_pending):
                    mwts = [bank_free[0]] if (mi == 0 and hm < 8) else []
                    if lo_m:
                        b.op("tensor", lambda e, h=hm:
                             e.matmul(B[0][:, h:h + 1], tau_lo[:, h * 128:(h + 1) * 128], onesc64[:],
                                      start=True, stop=False),
                             waits=mwts)
                        mwts = []
                    b.op("tensor", lambda e, h=hm, lo=lo_m:
                         e.matmul(B[0][:, h:h + 1], tau_hi[:, h * 128:(h + 1) * 128], onesc[:],
                                  start=(not lo), stop=True),
                         waits=mwts,
                         inc=pe if mi == len(mass_pending) - 1 else None)
                mass_pending = []
                pe_msg = pe.n
                pe_msg_prev = pe_msg
                pe_msg_all = pe_msg

                b.op("vector", lambda e: e.tensor_scalar_max(mass_sb[:], B[0][:, 0:16], 1e-6),
                     waits=[(pe, pe_msg)])
                b.drain("vector")
                b.op("vector", lambda e: e.reciprocal(recip_sb[:], mass_sb[:]), inc=ve)
                ve_recip = ve.n
                bank_free[0] = (ve, ve_recip)
                for h in range(NH):
                    b.op("scalar", lambda e, h=h:
                         e.activation(msg_n[:, h * 64:(h + 1) * 64],
                                      B[6 + h // 8][:, (h % 8) * 64:((h % 8) + 1) * 64],
                                      AF.Copy, scale=recip_sb[:, h:h + 1]),
                         waits=[(ve, ve_recip)] if h == 0 else (),
                         inc=se if h == NH - 1 else None)
                se_msgn = se.n
                bank_free[6] = (se, se_msgn)
                bank_free[7] = (se, se_msgn)
                se_msgn_prev = (se_msgn, qb)
            sm, pqb = se_msgn_prev
            se_msgn_prev = None
            for fc in range(NCC):
                b.op("tensor", lambda e, fc=fc:
                     e.transpose(B[1][:, fc * 64:fc * 64 + 64].bitcast(BF16),
                                 msg_n[:, fc * 128:(fc + 1) * 128], identB[:]),
                     waits=[(se, sm), bank_free[1]] if fc == 0 else (),
                     inc=pe if fc == NCC - 1 else None)
            pe_tr = pe.n
            for fc in range(NCC):
                b.op("vector", lambda e, fc=fc, pqb=pqb:
                     e.tensor_copy(msg_fm[:, fc * 512 + pqb * 128: fc * 512 + pqb * 128 + 128],
                                   B[1][:, fc * 64:fc * 64 + 64].bitcast(BF16)),
                     waits=[(pe, pe_tr)] if fc == 0 else (),
                     inc=ve if fc == NCC - 1 else None)
            bank_free[1] = (ve, ve.n)
            ve_msgfm = ve.n
            vtm_free = (pe, pe_msg_all)

            b.op("vector", lambda e:
                 e.tensor_copy(kA_halo[0:64].rearrange("p (c t) -> p c t", c=NCC),
                               kA[0:64].rearrange("p (c t) -> p c t", c=NCC)[:, :, 448:512]),
                 waits=[(pe, pe_msg_all)])
            b.op("vector", lambda e:
                 e.tensor_copy(kB_halo[0:64].rearrange("p (c t) -> p c t", c=NCC),
                               kB[0:64].rearrange("p (c t) -> p c t", c=NCC)[:, :, 448:512]))
            b.op("vector", lambda e: e.tensor_copy(v_lo[0:64, :], v_tm[64:128, 3 * 1024:4 * 1024]),
                 inc=ve)
            kv_halo_ready = (ve, ve.n)
            kbf_free = (ve, ve.n)

            if stages < 4:
                continue
            pe_wo_last = 0
            for fc in range(NCC):
                bk = fc % 2
                for cc in range(NCC):
                    b.op("tensor", lambda e, cc=cc, fc=fc, bk=bk:
                         e.matmul(B[bk][:], wo_bf[:, cc * 1024 + fc * 128: cc * 1024 + fc * 128 + 128],
                                  msg_fm[:, cc * 512:(cc + 1) * 512],
                                  start=(cc == 0), stop=(cc == NCC - 1)),
                         waits=[(ve, ve_msgfm), bank_free[bk]] if cc == 0 else (),
                         inc=pe if cc == NCC - 1 else None)
                pe_wo = pe.n
                pe_wo_last = pe_wo
                b.op("vector", lambda e, fc=fc, bk=bk:
                     e.tensor_tensor(ln1_32[:, fc * 512:(fc + 1) * 512], B[bk][:],
                                     ln1_32[:, fc * 512:(fc + 1) * 512], ALU.add),
                     waits=[(pe, pe_wo)], inc=ve)
                bank_free[bk] = (ve, ve.n)
            ve_pre = ve.n
            qbf_free = (pe, pe_wo_last)

            for cc in range(NCC):
                b.op("scalar", lambda e, cc=cc:
                     e.activation(arena_bf[:, cc * 512:(cc + 1) * 512], ln1_32[:, cc * 512:(cc + 1) * 512],
                                  AF.Square),
                     waits=[(ve, ve_pre)] if cc == 0 else (),
                     inc=se if cc == NCC - 1 else None)
            se_rsq = se.n
            for cc in range(NCC):
                b.op("tensor", lambda e, cc=cc:
                     e.matmul(B[4][0:1, :], onesc[:], arena_bf[:, cc * 512:(cc + 1) * 512],
                              start=(cc == 0), stop=(cc == NCC - 1)),
                     waits=[(se, se_rsq), bank_free[4]] if cc == 0 else (),
                     inc=pe if cc == NCC - 1 else None)
            pe_rstat = pe.n
            b.op("vector", lambda e: e.tensor_scalar(rows32[0:1, 0:512], B[4][0:1, :], 1.0 / D, None, ALU.mult),
                 waits=[(pe, pe_rstat)], inc=ve)
            b.op("scalar", lambda e: e.activation(rows32[0:1, 0:512], rows32[0:1, 0:512], AF.Ln, bias=epsrms[0:1, :]),
                 waits=[(ve, ve.n)])
            b.drain("scalar")
            b.op("scalar", lambda e: e.activation(rows32[0:1, 1024:1536], rows32[0:1, 0:512], AF.Exp, scale=-0.5),
                 inc=se)
            b.op("vector", lambda e: e.tensor_copy(rows_bf[0:1, 0:512], rows32[0:1, 1024:1536]),
                 waits=[(se, se.n)], inc=ve)
            ve_rr = ve.n
            b.op("tensor", lambda e: e.matmul(B[4][:], ones1[:], rows_bf[0:1, 0:512], start=True, stop=True),
                 waits=[(ve, ve_rr)], inc=pe)
            pe_rbc = pe.n
            for cc in range(NCC):
                b.op("vector", lambda e, cc=cc:
                     e.tensor_tensor(arena32[:, cc * 512:(cc + 1) * 512], ln1_32[:, cc * 512:(cc + 1) * 512],
                                     B[4][:], ALU.mult),
                     waits=[(pe, pe_rbc)] if cc == 0 else ())
            b.drain("vector")
            for cc in range(NCC):
                b.op("vector", lambda e, cc=cc, xo=xo:
                     e.scalar_tensor_tensor(ln1_32[:, cc * 512:(cc + 1) * 512],
                                            arena32[:, cc * 512:(cc + 1) * 512],
                                            gamma[:, cc:cc + 1],
                                            x_sb2[:, xo + cc * 512:xo + (cc + 1) * 512],
                                            ALU.mult, ALU.add),
                     inc=ve if cc == NCC - 1 else None)
            ve_x1 = ve.n
            ve_x1_prev = ve_x1
            xsb_free[i % 2] = (ve, ve_x1)
            bank_free[4] = (ve, ve_x1)

            x1_ts = b.dma_multi("scalar",
                                [(x1_dram[cc * 128:(cc + 1) * 128, t0:t0 + TILE],
                                  ln1_32[:, cc * 512:(cc + 1) * 512]) for cc in range(NCC)],
                                waits=[(ve, ve_x1)])
            x1_store_q.append(x1_ts)

            if stages < 5:
                continue
            se_stats0 = se.n
            for cc in range(NCC):
                b.op("scalar", lambda e, cc=cc:
                     e.copy(arena_bf[:, cc * 512:(cc + 1) * 512], ln1_32[:, cc * 512:(cc + 1) * 512]),
                     waits=[(ve, ve_x1)] if cc == 0 else ())
                b.op("scalar", lambda e, cc=cc:
                     e.activation(arena_bf[:, 4096 + cc * 512:4096 + (cc + 1) * 512],
                                  ln1_32[:, cc * 512:(cc + 1) * 512], AF.Square),
                     inc=se)
            se_stats = se.n
            pe_st2 = ln_stats()
            pe_st2_prev = pe_st2
            ve_rows2 = ln_rowops(pe_st2)
            b.op("tensor", lambda e: e.matmul(B[4][:], ones1[:], rows_bf[0:1, 0:512], start=True, stop=True),
                 waits=[(ve, ve_rows2)])
            b.op("tensor", lambda e: e.matmul(B[5][:], ones1[:], rows_bf[0:1, 512:1024], start=True, stop=True),
                 inc=pe)
            pe_bc2 = pe.n
            for cc in range(NCC):
                b.op("vector", lambda e, cc=cc:
                     e.tensor_tensor(ln1_bf[:, cc * 512:(cc + 1) * 512], ln1_32[:, cc * 512:(cc + 1) * 512],
                                     B[4][:], ALU.mult),
                     waits=([(pe, pe_bc2)] + ln1bf_free) if cc == 0 else ())
            b.drain("vector")
            for cc in range(NCC):
                b.op("vector", lambda e, cc=cc:
                     e.tensor_tensor(ln1_bf[:, cc * 512:(cc + 1) * 512], ln1_bf[:, cc * 512:(cc + 1) * 512],
                                     B[5][:], ALU.add),
                     inc=ve if cc == NCC - 1 else None)
            ve_ln2 = ve.n
            ve_ln2_prev = ve_ln2
            bank_free[4] = (ve, ve_ln2)
            bank_free[5] = (ve, ve_ln2)
            l2_ts = b.dma_multi("scalar",
                                [(ln2_dram[cc * 128:(cc + 1) * 128, t0:t0 + TILE],
                                  ln1_bf[:, cc * 512:(cc + 1) * 512]) for cc in range(NCC)],
                                waits=[(ve, ve_ln2)])

            for tb2 in range(4):
                for cc in range(NCC):
                    b.op("tensor", lambda e, cc=cc, tb2=tb2:
                         e.matmul(B[6][:, tb2 * 8:(tb2 + 1) * 8],
                                  ln1_32[:, cc * 512 + tb2 * 128: cc * 512 + tb2 * 128 + 128],
                                  wg32[:, cc * 8:(cc + 1) * 8],
                                  start=(cc == 0), stop=(cc == NCC - 1)),
                         waits=[(ve, ve_x1), bank_free[6]] if (tb2 == 0 and cc == 0) else (),
                         inc=pe if (tb2 == 3 and cc == NCC - 1) else None)
            pe_rt = pe.n
            for j in range(8):
                b.op("tensor", lambda e, j=j:
                     e.transpose(B[7][:, j:j + 1].bitcast(BF16)[:, 0:1],
                                 rows_bf[0:1, j * 128:(j + 1) * 128], identB[0:1, 0:1]),
                     waits=[(ve, ve_rows2), bank_free[7]] if j == 0 else (),
                     inc=pe if j == 7 else None)
            pe_ab = pe.n
            b.op("vector", lambda e:
                 e.tensor_copy(ab_tm[:], B[7][:, 0:8].bitcast(BF16)[:, 0:16:2]),
                 waits=[(pe, pe_ab)], inc=ve)
            bank_free[7] = (ve, ve.n)
            b.drain("vector")
            for tb2 in range(4):
                b.op("vector", lambda e, tb2=tb2:
                     e.tensor_scalar(r_sb[:, tb2 * 8:(tb2 + 1) * 8], B[6][:, tb2 * 8:(tb2 + 1) * 8],
                                     ab_tm[:, tb2:tb2 + 1], None, ALU.mult),
                     waits=[(pe, pe_rt)] if tb2 == 0 else ())
            b.drain("vector")
            for tb2 in range(4):
                b.op("vector", lambda e, tb2=tb2:
                     e.scalar_tensor_tensor(r_sb[:, tb2 * 8:(tb2 + 1) * 8], cs_rep[:],
                                            ab_tm[:, 4 + tb2:5 + tb2],
                                            r_sb[:, tb2 * 8:(tb2 + 1) * 8], ALU.mult, ALU.add),
                     inc=ve if tb2 == 3 else None)
            bank_free[6] = (ve, ve.n)
            b.drain("vector")
            for tb2 in range(4):
                b.op("vector", lambda e, tb2=tb2:
                     e.max(out=srt_sb[:, tb2 * 8:(tb2 + 1) * 8], in_=r_sb[:, tb2 * 8:(tb2 + 1) * 8]))
            b.drain("vector")
            for tb2 in range(4):
                b.op("vector", lambda e, tb2=tb2:
                     e.tensor_tensor(gd_sb[:, tb2:tb2 + 1], srt_sb[:, tb2 * 8 + 1:tb2 * 8 + 2],
                                     srt_sb[:, tb2 * 8:tb2 * 8 + 1], ALU.subtract),
                     inc=ve if tb2 == 3 else None)
            ve_gd = ve.n
            b.op("scalar", lambda e: e.activation(p2_sb[:], gd_sb[:], AF.Sigmoid), waits=[(ve, ve_gd)],
                 inc=se)
            se_gate = se.n
            b.op("vector", lambda e: e.tensor_scalar(pd_sb[:], p2_sb[:], -2.0, 1.0, ALU.mult, ALU.add),
                 waits=[(se, se_gate)])
            b.drain("vector")
            for tb2 in range(4):
                b.op("vector", lambda e, tb2=tb2:
                     e.tensor_scalar(wtmp1[:, tb2 * 8:(tb2 + 1) * 8], r_sb[:, tb2 * 8:(tb2 + 1) * 8],
                                     srt_sb[:, tb2 * 8:tb2 * 8 + 1], None, ALU.is_ge),
                     waits=[(se, se_gate)] if tb2 == 0 else ())
                b.op("vector", lambda e, tb2=tb2:
                     e.tensor_scalar(wtmp2[:, tb2 * 8:(tb2 + 1) * 8], r_sb[:, tb2 * 8:(tb2 + 1) * 8],
                                     srt_sb[:, tb2 * 8 + 1:tb2 * 8 + 2], None, ALU.is_ge))
            b.drain("vector")
            for tb2 in range(4):
                b.op("vector", lambda e, tb2=tb2:
                     e.tensor_scalar(wtmp1[:, tb2 * 8:(tb2 + 1) * 8], wtmp1[:, tb2 * 8:(tb2 + 1) * 8],
                                     pd_sb[:, tb2:tb2 + 1], None, ALU.mult))
            b.drain("vector")
            for tb2 in range(4):
                b.op("vector", lambda e, tb2=tb2:
                     e.scalar_tensor_tensor(w32[:, tb2 * 8:(tb2 + 1) * 8], wtmp2[:, tb2 * 8:(tb2 + 1) * 8],
                                            p2_sb[:, tb2:tb2 + 1], wtmp1[:, tb2 * 8:(tb2 + 1) * 8],
                                            ALU.mult, ALU.add))
            b.drain("vector")
            for tb2 in range(4):
                b.op("vector", lambda e, tb2=tb2:
                     e.tensor_copy(wstack[:, tb2 * 16:tb2 * 16 + 8], w32[:, tb2 * 8:(tb2 + 1) * 8]))
            b.drain("vector")
            for tb2 in range(4):
                b.op("vector", lambda e, tb2=tb2:
                     e.tensor_tensor(wstack[:, tb2 * 16 + 8:tb2 * 16 + 16], w32[:, tb2 * 8:(tb2 + 1) * 8],
                                     wstack[:, tb2 * 16:tb2 * 16 + 8], ALU.subtract),
                     inc=ve if tb2 == 3 else None)
            ve_ws = ve.n
            for tb2 in range(4):
                b.op("tensor", lambda e, tb2=tb2:
                     e.transpose(B[7][0:16, tb2 * 32:tb2 * 32 + 64].bitcast(BF16)[0:16, 0:128]
                                 if False else B[7][0:16, tb2 * 64:tb2 * 64 + 64].bitcast(BF16),
                                 wstack[:, tb2 * 16:(tb2 + 1) * 16], identB[:]),
                     waits=[(ve, ve_ws), bank_free[7]] if tb2 == 0 else (),
                     inc=pe if tb2 == 3 else None)
            pe_wt = pe.n
            for tb2 in range(4):
                b.op("vector", lambda e, tb2=tb2:
                     e.tensor_copy(w_fmt[:, tb2 * 128:(tb2 + 1) * 128],
                                   B[7][0:16, tb2 * 64:tb2 * 64 + 64].bitcast(BF16)),
                     waits=([(pe, pe_wt)] + w_fmt_free) if tb2 == 0 else (),
                     inc=ve if tb2 == 3 else None)
            bank_free[7] = (ve, ve.n)
            w_ts = b.dma_multi("scalar", [(w_dram[:, t0:t0 + TILE], w_fmt[:])], waits=[(ve, ve.n)])
            snap = b.lane_snap("scalar")
            w_fmt_free = snap
            ln1bf_free = snap
            ln132_free = snap
            stores.append(tuple(snap))
            w_stores.append(snap)

        ve_A = ve.n
        se_A = se.n
        pe_A = pe.n
        qs_A = b.lane_snap("scalar")

        if debug:
            b.dma("scalar", x1_dbg, x1_dram, waits=qs_A)
            b.dma("scalar", ln2_dbg, ln2_dram)
            b.dma("scalar", w_dbg, w_dram, waits=w_stores[-1] if w_stores else [])

        stA.close()
        stC = ExitStack()
        ctx.enter_context(stC)
        sbC = lambda n, sh, dt: stC.enter_context(nc.sbuf_tensor(n, sh, dt))
        ln2_q = sbC("ln2_q", [128, 2 * NCC * QT], BF16)
        y_q = sbC("y_q", [128, NCC * QT], F32)
        w1b = sbC("w1b", [128, 2 * NCC * HE], BF16)
        w2b = sbC("w2b", [128, NCC * HE], BF16)
        w3b = sbC("w3b", [128, NHC * 1024], BF16)
        h_w = sbC("h_w", [128, 2 * NHC * 512], BF16)
        bcast_sb = sbC("bcast_sb", [128, QT], BF16)
        w_q = sbC("w_q", [16, 2 * QT], BF16)
        silu_g = sbC("silu_g", [128, 2 * 512], BF16)
        b_w = sbC("b_w", [128, 2 * 512], BF16)

        phaseA_done = [(ve, ve_A), (se, se_A), (pe, pe_A)] + b.lane_snap("scalar") + b.lane_snap("gpsimd")
        for eng in ENGS:
            b.waits(eng, phaseA_done)

        def w_srcs(e_):
            if e_ < NE:
                return (w1_d[e_].rearrange("(c p) h -> p c h", p=128),
                        w2_d[e_].rearrange("(c p) h -> p c h", p=128),
                        w3_d[e_].rearrange("(c p) h -> p c h", p=128))
            s = (e_ - NE) * HE
            return (ws1_d[:, s:s + HE].rearrange("(c p) h -> p c h", p=128),
                    ws2_d[:, s:s + HE].rearrange("(c p) h -> p c h", p=128),
                    ws3_d[s:s + HE, :].rearrange("(c p) h -> p c h", p=128))

        w1_ready = {}
        w2_ready = {}
        w3_ready = {}
        bank_free = {i: (pe, 0) for i in range(8)}
        ve_bcast = {}
        ystore = []
        hw_free = {0: (pe, 0), 1: (pe, 0)}
        yq_free = []
        lnq_par_free = {0: (pe, 0), 1: (pe, 0)}
        wq_free = {0: (pe, 0), 1: (pe, 0)}
        ln2_loaded = {}
        s1_free_val = {0: 0, 1: 0}

        seq = [(qq, ee) for qq in range(n_q) for ee in range(NPE)]
        cache_st = {}

        def prefetch(qn, en, npar, w1free, w2free, w3free):
            if qn == 0:
                g1, g2, g3 = w_srcs(en)
                hh = HE // 2
                d1 = w1b[:, npar * NCC * HE:(npar + 1) * NCC * HE].rearrange("p (c h) -> p c h", c=NCC)
                d2 = w2b[:].rearrange("p (c h) -> p c h", c=NCC)
                w1_ready[(qn, en)] = [
                    b.dma("gpsimd", d1[:, :, 0:hh], g1[:, :, 0:hh], waits=w1free),
                    b.dma("gpsimd", d1[:, :, hh:HE], g1[:, :, hh:HE], waits=w1free)]
                w2_ready[(qn, en)] = [
                    b.dma("gpsimd", d2[:, :, 0:hh], g2[:, :, 0:hh], waits=w2free),
                    b.dma("gpsimd", d2[:, :, hh:HE], g2[:, :, hh:HE], waits=w2free)]
                w3_ready[(qn, en)] = [b.dma(
                    "gpsimd", w3b[:].rearrange("p (c h) -> p c h", c=NHC), g3, waits=w3free)]
                sts = []
                sts.append(b.dma("sync", w1c[en], w1b[:, npar * NCC * HE:(npar + 1) * NCC * HE],
                                 waits=w1_ready[(qn, en)]))
                sts.append(b.dma("sync", w2c[en], w2b[:], waits=w2_ready[(qn, en)]))
                sts.append(b.dma("sync", w3c[en], w3b[:], waits=w3_ready[(qn, en)]))
                cache_st[en] = sts
            else:
                w1_ready[(qn, en)] = [b.dma("sync", w1b[:, npar * NCC * HE:(npar + 1) * NCC * HE],
                                            w1c[en], waits=w1free + (cache_st[en][0:1] if qn == 1 else []))]
                w2_ready[(qn, en)] = [b.dma("sync", w2b[:], w2c[en],
                                            waits=w2free + (cache_st[en][1:2] if qn == 1 else []))]
                w3_ready[(qn, en)] = [b.dma("sync", w3b[:], w3c[en],
                                            waits=w3free + (cache_st[en][2:3] if qn == 1 else []))]

        wq_loaded = {}
        if n_q > 0:
            ln2_loaded[0] = b.dma("sync",
                                  ln2_q[:, 0:NCC * QT].rearrange("p (c t) -> p c t", c=NCC),
                                  ln2_dram[:, 0:QT].rearrange("(c p) t -> p c t", p=128),
                                  waits=list(stores[1]))
            wq_loaded[0] = b.dma("sync", w_q[:, 0:QT], w_dram[:, 0:QT], waits=list(w_stores[1]))
            prefetch(0, 0, 0, [], [], [])
        for qq in range(n_q):
            tq = qq * QT
            lq = (qq % 2) * NCC * QT
            if qq + 1 < n_q:
                tn = (qq + 1) * QT
                wq_loaded[qq + 1] = b.dma("sync" if qq == 0 else "gpsimd",
                                          w_q[:, ((qq + 1) % 2) * QT:((qq + 1) % 2 + 1) * QT],
                                          w_dram[:, tn:tn + QT],
                                          waits=[wq_free[(qq + 1) % 2]] +
                                                list(w_stores[min(2 * qq + 3, len(w_stores) - 1)]))
                ln2_loaded[qq + 1] = b.dma(
                    "sync" if qq == 0 else "gpsimd",
                    ln2_q[:, ((qq + 1) % 2) * NCC * QT:((qq + 1) % 2 + 1) * NCC * QT]
                    .rearrange("p (c t) -> p c t", c=NCC),
                    ln2_dram[:, tn:tn + QT].rearrange("(c p) t -> p c t", p=128),
                    waits=[lnq_par_free[(qq + 1) % 2]] + list(stores[min(2 * qq + 3, len(stores) - 1)]))
            y_loaded = b.dma("sync" if qq == 0 else "gpsimd", y_q[:].rearrange("p (c t) -> p c t", c=NCC),
                             x1_dram[:, tq:tq + QT].rearrange("(c p) t -> p c t", p=128),
                             waits=yq_free + list(stores[2 * qq + 1]))

            for ee in range(NPE):
                idx = qq * NPE + ee
                w1par = idx % 2
                if ee < NE:
                    for tb in range(2):
                        b.op("tensor", lambda e, ee=ee, tb=tb, qq=qq:
                             e.matmul(B[6 + tb][:], selW[:, ee * 128:(ee + 1) * 128],
                                      w_q[:, (qq % 2) * QT + tb * TB: (qq % 2) * QT + (tb + 1) * TB],
                                      start=True, stop=True),
                             waits=[bank_free[6 + tb]] + ([wq_loaded[qq]] if (ee == 0 and tb == 0) else []),
                             inc=pe)
                        pe_b = pe.n
                        b.op("vector", lambda e, tb=tb:
                             e.tensor_copy(bcast_sb[:, tb * TB:(tb + 1) * TB], B[6 + tb][:]),
                             waits=[(pe, pe_b)], inc=ve)
                        bank_free[6 + tb] = (ve, ve.n)
                    ve_bcast[(qq, ee)] = (ve, ve.n)

                pe_hc = {}
                po_hc = {}
                po_h_tb = {}
                for tb in range(2):
                    for hc in range(NHC):
                        ghc = tb * NHC + hc
                        gb = ghc % 3
                        gbank = B[2 * gb]
                        bbank = B[2 * gb + 1]
                        wts = [bank_free[2 * gb], bank_free[2 * gb + 1]]
                        if ghc == 0:
                            wts += w1_ready[(qq, ee)] + w2_ready[(qq, ee)] + [ln2_loaded[qq]]
                        for cc in range(NCC):
                            b.op("tensor", lambda e, cc=cc, hc=hc, tb=tb, w1par=w1par, gbank=gbank, lq=lq:
                                 e.matmul(gbank[:],
                                          w1b[:, w1par * NCC * HE + cc * HE + hc * 128:
                                              w1par * NCC * HE + cc * HE + hc * 128 + 128],
                                          ln2_q[:, lq + cc * QT + tb * TB: lq + cc * QT + tb * TB + TB],
                                          start=(cc == 0), stop=(cc == NCC - 1)),
                                 waits=wts if cc == 0 else ())
                            wts = ()
                        for cc in range(NCC):
                            b.op("tensor", lambda e, cc=cc, hc=hc, tb=tb, bbank=bbank, lq=lq:
                                 e.matmul(bbank[:],
                                          w2b[:, cc * HE + hc * 128: cc * HE + hc * 128 + 128],
                                          ln2_q[:, lq + cc * QT + tb * TB: lq + cc * QT + tb * TB + TB],
                                          start=(cc == 0), stop=(cc == NCC - 1)),
                                 inc=pe if cc == NCC - 1 else None)
                        pe_hc[ghc] = pe.n
                        sl = ghc % 2
                        wts_s = [(pe, pe_hc[ghc])]
                        if ghc >= 2:
                            wts_s.append((po, po_hc[ghc - 2]))
                        b.op("scalar", lambda e, sl=sl, gbank=gbank:
                             e.activation(silu_g[:, sl * 512:(sl + 1) * 512], gbank[:], AF.Silu),
                             waits=wts_s, inc=se)
                        se_silu = se.n
                        wts_v = [(pe, pe_hc[ghc])]
                        if ghc >= 2:
                            wts_v.append((po, po_hc[ghc - 2]))
                        if ee < NE:
                            b.op("vector", lambda e, tb=tb, sl=sl, bbank=bbank:
                                 e.tensor_tensor(b_w[:, sl * 512:(sl + 1) * 512], bbank[:],
                                                 bcast_sb[:, tb * TB:(tb + 1) * TB], ALU.mult),
                                 waits=wts_v + [ve_bcast[(qq, ee)]], inc=ve)
                        else:
                            b.op("vector", lambda e, sl=sl, bbank=bbank:
                                 e.tensor_copy(b_w[:, sl * 512:(sl + 1) * 512], bbank[:]),
                                 waits=wts_v, inc=ve)
                        ve_bw = ve.n
                        wts_p = [(se, se_silu), (ve, ve_bw)]
                        if hc == 0:
                            wts_p.append(hw_free[tb])
                        b.op("gpsimd", lambda e, hc=hc, sl=sl, tb=tb:
                             e.tensor_tensor(h_w[:, tb * NHC * 512 + hc * 512: tb * NHC * 512 + (hc + 1) * 512],
                                             silu_g[:, sl * 512:(sl + 1) * 512],
                                             b_w[:, sl * 512:(sl + 1) * 512], ALU.mult),
                             waits=wts_p, inc=po)
                        po_hc[ghc] = po.n
                        bank_free[2 * gb] = (se, se_silu)
                        bank_free[2 * gb + 1] = (ve, ve_bw)
                    po_h_tb[tb] = po.n
                s1_free_val[w1par] = pe_hc[2 * NHC - 1]

                for tb in range(2):
                    for fc in range(NCC):
                        yb = B[6 + fc % 2]
                        wts = [bank_free[6 + fc % 2]]
                        if fc == 0:
                            wts += [(po, po_h_tb[tb]), y_loaded] + w3_ready[(qq, ee)]
                        for hc in range(NHC):
                            b.op("tensor", lambda e, hc=hc, fc=fc, yb=yb, tb=tb:
                                 e.matmul(yb[:], w3b[:, hc * 1024 + fc * 128: hc * 1024 + fc * 128 + 128],
                                          h_w[:, tb * NHC * 512 + hc * 512: tb * NHC * 512 + (hc + 1) * 512],
                                          start=(hc == 0), stop=(hc == NHC - 1)),
                                 waits=wts if hc == 0 else (),
                                 inc=pe if hc == NHC - 1 else None)
                            wts = ()
                        pe_y = pe.n
                        b.op("vector", lambda e, fc=fc, tb=tb, yb=yb:
                             e.tensor_tensor(y_q[:, fc * QT + tb * TB: fc * QT + tb * TB + TB], yb[:],
                                             y_q[:, fc * QT + tb * TB: fc * QT + tb * TB + TB], ALU.add),
                             waits=[(pe, pe_y)], inc=ve)
                        bank_free[6 + fc % 2] = (ve, ve.n)
                    hw_free[tb] = (pe, pe.n)
                s2_done = ve.n

                if idx + 1 < len(seq):
                    qn, en = seq[idx + 1]
                    npar = (idx + 1) % 2
                    w1free = [(pe, s1_free_val[npar])]
                    w2free = [(pe, s1_free_val[w1par])]
                    w3free = [(ve, s2_done)]
                    if idx >= 1 and seq[idx - 1][1] in cache_st:
                        w1free += cache_st[seq[idx - 1][1]][0:1]
                    if ee in cache_st:
                        w2free += cache_st[ee][1:2]
                        w3free += cache_st[ee][2:3]
                    prefetch(qn, en, npar, w1free, w2free, w3free)

            y_ts = b.dma_multi("scalar",
                               [(out_fm[cc * 128:(cc + 1) * 128, tq:tq + QT],
                                 y_q[:, cc * QT:(cc + 1) * QT]) for cc in range(NCC)],
                               waits=[(ve, s2_done)])
            snap = b.lane_snap("scalar")
            ystore.append(snap)
            yq_free = snap
            lnq_par_free[qq % 2] = (pe, pe.n)
            wq_free[qq % 2] = (pe, pe.n)

        b.waits("gpsimd", b.lane_snap("scalar") + b.lane_snap("sync") + b.lane_snap("gpsimd"))

        with nc.Block() as block:
            @block.sync
            def _(eng):
                for f in b.q["sync"]:
                    f(eng)

            @block.tensor
            def _(eng):
                for f in b.q["tensor"]:
                    f(eng)

            @block.vector
            def _(eng):
                for f in b.q["vector"]:
                    f(eng)

            @block.scalar
            def _(eng):
                for f in b.q["scalar"]:
                    f(eng)

            @block.gpsimd
            def _(eng):
                for f in b.q["gpsimd"]:
                    f(eng)

    return nc


def host_constants():
    half = HD // 2
    theta = 1.0 / (10000.0 ** (np.arange(half, dtype=np.float64) / half))
    freqs = np.arange(T, dtype=np.float64)[:, None] * theta[None, :]
    cosf = np.cos(freqs)
    sinf = np.sin(freqs)
    p = np.arange(128)
    ii = p % 64
    idx = np.where(ii < 32, ii, ii - 32)
    sign = np.where(ii < 32, -1.0, 1.0)
    C128 = np.ascontiguousarray(cosf[:, idx].T).astype(BF)
    S128 = np.ascontiguousarray((sign[None, :] * sinf[:, idx]).T).astype(BF)
    permP = np.zeros((128, 128), dtype=BF)
    partner = np.where(ii < 32, p + 32, p - 32)
    permP[p, partner] = 1.0
    identB = np.eye(128, dtype=BF)
    selAB = np.zeros((2, 256), dtype=BF)
    selAB[0, 0:128] = 1.0
    selAB[1, 128:256] = 1.0
    selW = np.zeros((16, NE * 128), dtype=BF)
    for e_ in range(NE):
        selW[e_, e_ * 128:(e_ + 1) * 128] = 1.0
        selW[NE + e_, e_ * 128:(e_ + 1) * 128] = 1.0
    jj = np.arange(128)[:, None]
    ti = np.arange(128)[None, :]
    mhi = ((jj + 1 <= ti) & (ti <= jj + 64)).astype(BF)
    jl = np.arange(64)[:, None]
    mlo = (np.arange(128)[None, :] <= jl).astype(BF)
    maskhi2 = np.ascontiguousarray(np.concatenate([mhi] * 4, axis=1))
    masklo2 = np.ascontiguousarray(np.concatenate([mlo] * 4, axis=1))
    return dict(cosT=C128, sinT=S128, permP=permP, identB=identB, selAB=selAB, selW=selW,
                maskhi2=maskhi2, masklo2=masklo2)


_cache = {}


def kernel(L0_attn_x, L0_attn_wq, L0_attn_wk, L0_attn_wv, L0_attn_wo, L0_attn_norm_scale,
           L0_ffn_Wg, L0_ffn_W1, L0_ffn_W2, L0_ffn_W3, L0_ffn_Ws1, L0_ffn_Ws2, L0_ffn_Ws3):
    x = np.ascontiguousarray(np.asarray(L0_attn_x, dtype=np.float32))
    gamma = np.asarray(L0_attn_norm_scale, dtype=np.float32)
    gamma_fm = np.ascontiguousarray(gamma.reshape(NCC, 128).T)
    shared = dict(
        wq=np.ascontiguousarray(np.asarray(L0_attn_wq, np.float32)),
        wk=np.ascontiguousarray(np.asarray(L0_attn_wk, np.float32)),
        wv=np.ascontiguousarray(np.asarray(L0_attn_wv, np.float32)),
        wo=np.ascontiguousarray(np.asarray(L0_attn_wo, np.float32)),
        gamma_fm=gamma_fm,
        wg=np.ascontiguousarray(np.asarray(L0_ffn_Wg, np.float32)),
        w1=np.ascontiguousarray(np.asarray(L0_ffn_W1, np.float32)),
        w2=np.ascontiguousarray(np.asarray(L0_ffn_W2, np.float32)),
        w3=np.ascontiguousarray(np.asarray(L0_ffn_W3, np.float32)),
        ws1=np.ascontiguousarray(np.asarray(L0_ffn_Ws1, np.float32)),
        ws2=np.ascontiguousarray(np.asarray(L0_ffn_Ws2, np.float32)),
        ws3=np.ascontiguousarray(np.asarray(L0_ffn_Ws3, np.float32)),
        **host_constants(),
    )
    if "nc" not in _cache:
        _cache["nc"] = build_program()
    nc = _cache["nc"]
    in_maps = []
    for b_ in range(8):
        m = dict(shared)
        m["x_fm"] = np.ascontiguousarray(x[b_].T)
        in_maps.append(m)
    res = run_bass_kernel_spmd(nc, in_maps, core_ids=list(range(8)))
    out = np.empty((8, T, D), dtype=np.float32)
    for b_ in range(8):
        out[b_] = np.asarray(res.results[b_]["out_fm"], dtype=np.float32).T
    return out
